# version 11
# baseline (speedup 1.0000x reference)
import numpy as np
import ml_dtypes
from contextlib import ExitStack

import concourse.bass as bass
import concourse.mybir as mybir
from concourse.bass_utils import run_bass_kernel_spmd
from concourse import dve_ops as DVO

F32 = mybir.dt.float32
BF16 = mybir.dt.bfloat16
FP8 = mybir.dt.float8e4
U8 = mybir.dt.uint8
I32 = mybir.dt.int32
AF = mybir.ActivationFunctionType
ALU = mybir.AluOpType
AX = mybir.AxisListType

H, DH, D = 16, 128, 2048
TC, NT, NCH, NJ = 2048, 16, 16, 16
EPS_LN = 1e-5
EPS32 = float(np.finfo(np.float32).eps)
ASCALE = float(np.float32(1.0) / np.sqrt(np.float32(128.0)))
OSCALE = float(np.float32(128.0 ** -0.5))

STAGE = 4


class Sem:
    def __init__(self, st, nc, name):
        self.h = st.enter_context(nc.semaphore(name))
        self.n = 0

    def inc(self, ins, k=1):
        ins.then_inc(self.h, k)
        self.n += k
        return ins


def build(stage=STAGE):
    nc = bass.Bass("TRN2", target_bir_lowering=False)

    def din(name, shape, dt=F32):
        return nc.dram_tensor(name, shape, dt, kind="ExternalInput").ap()

    xo = din("xo", [TC, D])
    xr = din("xr", [TC, D])
    wq_t = din("wq_t", [H, NCH, 128, 128])
    wkv_t = din("wkv_t", [NCH, 128, 256])
    wo_t = din("wo_t", [4, NCH, 128, 512])
    bo = din("bo", [128, D])
    qkv_t = din("qkv_t", [H, NCH, 128, 384])
    proj_t = din("proj_t", [4, NCH, 128, 512])
    f8_t = din("f8_t", [4, NCH, 128, 512], FP8)
    wscale = din("wscale", [128, 1])
    iscale = din("iscale", [128, 1], BF16)
    s_cols = din("s_cols", [128, H])
    qnw = din("qnw", [128, NCH])
    nw = din("nw", [128, NCH])
    pnw = din("pnw", [128, NCH])
    smask = din("smask", [128, 1])

    out_hbm = nc.dram_tensor("out", [TC, D], F32, kind="ExternalOutput").ap()

    ctx_h = nc.dram_tensor("ctx_h", [NT, H, 128, 128], BF16).ap()
    qkv_h = nc.dram_tensor("qkv_h", [H, 4, 128, TC], BF16).ap()
    x2_h = nc.dram_tensor("x2_h", [NT, 128, D], F32).ap()
    x3_h = nc.dram_tensor("x3_h", [NT, 128, D], F32).ap()
    fw_bf = nc.dram_tensor("fw_bf", [4, 128, NCH * 512], BF16).ap()
    cc_in = nc.dram_tensor("cc_in", [128, H * 128], F32)
    cc_out = nc.dram_tensor("cc_out", [2, 128, H * 128], F32)

    xo_t = xo.rearrange("(t p) d -> t p d", p=128)
    xr_t = xr.rearrange("(t p) d -> t p d", p=128)
    out_t = out_hbm.rearrange("(t p) d -> t p d", p=128)

    st = ExitStack()
    with st:
        E = st.enter_context
        sIN = Sem(st, nc, "sIN")
        sSP = Sem(st, nc, "sSP")
        sV = Sem(st, nc, "sV")
        sS = Sem(st, nc, "sS")
        sT = Sem(st, nc, "sT")
        sG = Sem(st, nc, "sG")
        sCC = Sem(st, nc, "sCC")
        sGD = Sem(st, nc, "sGD")
        sXL = [Sem(st, nc, f"sXL{i}") for i in range(2)]
        sCX = [Sem(st, nc, f"sCX{i}") for i in range(2)]
        sWQ = [Sem(st, nc, f"sWQ{i}") for i in range(2)]
        sWO = Sem(st, nc, "sWO")
        sXP = Sem(st, nc, "sXP")
        sOUT = Sem(st, nc, "sOUT")
        sAB = [Sem(st, nc, f"sAB{i}") for i in range(2)]

        blk = E(nc.Block())

        SCRAP = E(nc.sbuf_tensor("SCRAP", [128, 2048], F32))
        HBF = E(nc.sbuf_tensor("HBF", [128, 2048], BF16))
        STATA = E(nc.sbuf_tensor("STATA", [128, 16], F32))
        STATB = E(nc.sbuf_tensor("STATB", [128, 16], F32))
        NSC = E(nc.sbuf_tensor("NSC", [128, 16], F32))
        NBI = E(nc.sbuf_tensor("NBI", [128, 16], F32))
        TMPA = E(nc.sbuf_tensor("TMPA", [128, 16], F32))
        TMPB = E(nc.sbuf_tensor("TMPB", [128, 16], F32))
        TMPC = E(nc.sbuf_tensor("TMPC", [128, 16], F32))
        TMPD = E(nc.sbuf_tensor("TMPD", [128, 16], F32))
        TMPE = E(nc.sbuf_tensor("TMPE", [128, 16], F32))
        ONESB = E(nc.sbuf_tensor("ONESB", [128, 1], BF16))
        ONES1F = E(nc.sbuf_tensor("ONES1F", [1, 128], F32))
        ONES1B = E(nc.sbuf_tensor("ONES1B", [1, 128], BF16))
        IDENT = E(nc.sbuf_tensor("IDENT", [128, 128], BF16))
        IDENTF = E(nc.sbuf_tensor("IDENTF", [128, 128], F32))
        CONSC = E(nc.sbuf_tensor("CONSC", [128, 8], F32))
        SCL = E(nc.sbuf_tensor("SCL", [128, 8], F32))
        ISC = E(nc.sbuf_tensor("ISC", [128, 1], BF16))
        SCOL = E(nc.sbuf_tensor("SCOL", [128, H], F32))
        QNW = E(nc.sbuf_tensor("QNW", [128, NCH], F32))
        NW = E(nc.sbuf_tensor("NW", [128, NCH], F32))
        PNW = E(nc.sbuf_tensor("PNW", [128, NCH], F32))
        W12S = E(nc.sbuf_tensor("W12S", [128, NCH], F32))
        W1SQ = E(nc.sbuf_tensor("W1SQ", [128, 2 * H], BF16))
        SMK = E(nc.sbuf_tensor("SMK", [128, 1], F32))
        MASKT = E(nc.sbuf_tensor("MASKT", [128, H * 128], BF16))
        LAMQ = E(nc.sbuf_tensor("LAMQ", [128, H * 128], BF16))
        LAMRV = E(nc.sbuf_tensor("LAMRV", [128, H], F32))
        LAMTT = E(nc.sbuf_tensor("LAMTT", [128, H], F32))
        LAMJ = E(nc.sbuf_tensor("LAMJ", [128, H * NJ], F32))
        ABSD = E(nc.sbuf_tensor("ABSD", [128, 16], F32))
        ABSE = E(nc.sbuf_tensor("ABSE", [128, 8], F32))
        DIFFT = E(nc.sbuf_tensor("DIFFT", [128, 128], F32))
        INDC = E(nc.sbuf_tensor("INDC", [128, 128], F32))


        class AutoV:
            def __init__(self, v):
                self._v = v

            def __getattr__(self, k):
                f = getattr(self._v, k)
                def g(*a, **kw):
                    r = f(*a, **kw)
                    sV.inc(self._v.engine_nop())
                    self._v.wait_ge(sV.h, sV.n)
                    return r
                return g

        def rsqrt(v, dst, src, t1, t2):
            di = dst.bitcast(I32)
            si = src.bitcast(I32)
            v.tensor_scalar(di, si, 1, None, ALU.logical_shift_right)
            v.tensor_scalar(di, di, -1, 0x5F3759DF, ALU.mult, ALU.add)
            for _ in range(3):
                v.tensor_tensor(t1, dst, dst, ALU.mult)
                v.tensor_tensor(t1, t1, src, ALU.mult)
                v.tensor_scalar(t1, t1, -0.5, 1.5, ALU.mult, ALU.add)
                v.tensor_tensor(dst, dst, t1, ALU.mult)

        @blk.sync
        def _(sy):
            sIN.inc(sy.dma_start(out=SCOL[:, :], in_=s_cols), 16)
            sIN.inc(sy.dma_start(out=QNW[:, :], in_=qnw), 16)
            sIN.inc(sy.dma_start(out=NW[:, :], in_=nw), 16)
            sIN.inc(sy.dma_start(out=PNW[:, :], in_=pnw), 16)
            sIN.inc(sy.dma_start(out=SMK[:, :], in_=smask), 16)
            sIN.inc(sy.dma_start(out=ISC[:, :], in_=iscale), 16)
            sIN.inc(sy.dma_start(out=SCL[:, 2:3], in_=wscale), 16)
        p0_in = sIN.n

        @blk.gpsimd
        def _(g):
            g.memset(ONESB[:, :], 1.0)
            g.memset(ONES1F[:, :], 1.0)
            g.memset(ONES1B[:, :], 1.0)
            g.memset(CONSC[:, 0:1], -0.5)
            g.memset(CONSC[:, 1:2], 0.5)
            g.memset(CONSC[:, 2:3], EPS_LN)
            g.memset(CONSC[:, 3:4], EPS32)
            g.memset(CONSC[:, 4:5], 1e-6)
            sG.inc(g.memset(IDENT[:, :], 0.0))
            sG.inc(g.memset(IDENTF[:, :], 0.0))
            sG.inc(g.memset(INDC[:, :], 1.0))
            g.wait_ge(sG.h, sG.n)
            g.affine_select(out=IDENTF[:, :], in_=IDENTF[:, :],
                            compare_op=ALU.not_equal, fill=1.0, base=0,
                            pattern=[[-1, 128]], channel_multiplier=1)
            g.affine_select(out=IDENT[:, :], in_=IDENT[:, :],
                            compare_op=ALU.not_equal, fill=1.0, base=0,
                            pattern=[[-1, 128]], channel_multiplier=1)
            g.iota(DIFFT[:, :], pattern=[[1, 128]], base=0, channel_multiplier=-1,
                   allow_small_or_imprecise_dtypes=True)
            g.affine_select(out=INDC[:, :], in_=INDC[:, :],
                            compare_op=ALU.is_ge, fill=0.0, base=0,
                            pattern=[[1, 128]], channel_multiplier=-1)
            sG.inc(g.iota(SCRAP[:, 0:H], pattern=[[0, H]], base=0,
                          channel_multiplier=1, allow_small_or_imprecise_dtypes=True))
        p0_g = sG.n

        @blk.vector
        def _(v):
            v.wait_ge(sIN.h, p0_in)
            v.wait_ge(sG.h, p0_g)
            v = AutoV(v)
            v.tensor_copy(SCL[:, 0:1], ISC[:, :])
            v.tensor_tensor(SCL[:, 1:2], SCL[:, 0:1], SCL[:, 2:3], ALU.mult)
            v.reciprocal(SCL[:, 0:1], SCL[:, 0:1])
            v.tensor_tensor(W12S[:, :], NW[:, :], PNW[:, :], ALU.mult)
            v.tensor_tensor(TMPA[:, 0:16], NW[:, :], NW[:, :], ALU.mult)
            for h in range(H):
                v.tensor_copy(W1SQ[:, 2*h:2*h+1], ONESB[:, 0:1])
                v.tensor_copy(W1SQ[:, 2*h+1:2*h+2], TMPA[:, h:h+1])
            v.tensor_scalar(TMPB[:, 0:H], SCRAP[:, 0:H], -128.0, None, ALU.add)
            v.tensor_tensor(TMPB[:, 0:H], TMPB[:, 0:H], SCOL[:, :], ALU.mult)
            v.tensor_scalar(TMPC[:, 0:H], SCOL[:, :], -128.0, None, ALU.mult)
            sV.inc(v.engine_nop())
        p0_v = sV.n

        @blk.scalar
        def _(s):
            s.preload_act_table(AF.Exp)
            s.wait_ge(sV.h, p0_v)
            s.activation(LAMRV[:, :], TMPB[:, 0:H], AF.Exp)
            sS.inc(s.activation(LAMTT[:, :], TMPC[:, 0:H], AF.Exp))
        p0_s = sS.n

        @blk.vector
        def _(v):
            v.wait_ge(sS.h, p0_s)
            v = AutoV(v)
            dif_b = DIFFT[:, :].unsqueeze(1).broadcast_to([128, H, 128])
            sc_b = SCOL[:, :].unsqueeze(2).broadcast_to([128, H, 128])
            m3 = SCRAP[:, :].rearrange("p (h t) -> p h t", t=128)
            v.scalar_tensor_tensor(m3, dif_b, -1.0, sc_b, ALU.mult, ALU.mult)
        p0_v2 = sV.n

        @blk.scalar
        def _(s):
            s.wait_ge(sV.h, p0_v2)
            sS.inc(s.activation(SCRAP[:, :], SCRAP[:, :], AF.Exp))
        p0_s2 = sS.n

        @blk.vector
        def _(v):
            v.wait_ge(sS.h, p0_s2)
            for h in range(H):
                v.tensor_tensor(MASKT[:, h*128:(h+1)*128], SCRAP[:, h*128:(h+1)*128],
                                INDC[:, :], ALU.mult)
            sV.inc(v.engine_nop())
            v.wait_ge(sV.h, sV.n)
        p0_v3 = sV.n

        @blk.gpsimd
        def _(g):
            g.wait_ge(sV.h, p0_v3)
            sG.inc(g.iota(SCRAP[:, :], pattern=[[0, H], [1, 128]], base=0,
                          channel_multiplier=0, allow_small_or_imprecise_dtypes=True))
        p0_g2 = sG.n

        @blk.vector
        def _(v):
            v.wait_ge(sG.h, p0_g2)
            v = AutoV(v)
            t3 = SCRAP[:, :].rearrange("p (h t) -> p h t", t=128)
            sc_b = SCOL[:, :].unsqueeze(2).broadcast_to([128, H, 128])
            v.scalar_tensor_tensor(t3, t3, -1.0, sc_b, ALU.mult, ALU.mult)
        p0_v4 = sV.n

        @blk.scalar
        def _(s):
            s.wait_ge(sV.h, p0_v4)
            sS.inc(s.activation(LAMQ[:, :], SCRAP[:, :], AF.Exp))
        p0_done = sS.n

        @blk.gpsimd
        def _(g):
            sG.inc(g.iota(LAMJ[:, :], pattern=[[0, H], [1, NJ]], base=0,
                          channel_multiplier=0,
                          allow_small_or_imprecise_dtypes=True))
        lamj_g = sG.n

        @blk.vector
        def _(v, lamj_g=lamj_g):
            v.wait_ge(sG.h, lamj_g)
            v.wait_ge(sS.h, p0_done)
            v = AutoV(v)
            v.tensor_scalar(LAMJ[:, :], LAMJ[:, :], 128.0, None, ALU.mult)
            v.tensor_scalar(LAMJ[:, :], LAMJ[:, :], -2048.0 + 128.0, None, ALU.add)
            l3 = LAMJ[:, :].rearrange("p (h j) -> p h j", j=NJ)
            sc_b = SCOL[:, :].unsqueeze(2).broadcast_to([128, H, NJ])
            v.scalar_tensor_tensor(l3, l3, 1.0, sc_b, ALU.mult, ALU.mult)
        lamj_v = sV.n

        @blk.scalar
        def _(s, lamj_v=lamj_v):
            s.wait_ge(sV.h, lamj_v)
            sS.inc(s.activation(LAMJ[:, :], LAMJ[:, :], AF.Exp))
        p0_done = sS.n

        def ln_stats_tile(v, x_ap, i):
            sV.inc(v.scalar_tensor_tensor(SCRAP[:, :], x_ap, 1.0, x_ap, ALU.mult,
                                          ALU.mult, accum_out=STATA[:, i:i+1]))
            v.wait_ge(sV.h, sV.n)
            sV.inc(v.tensor_scalar(SCRAP[:, :], x_ap, 0.0, 0.0, ALU.add, ALU.add,
                                   accum_out=STATB[:, i:i+1]))
            v.wait_ge(sV.h, sV.n)

        def ln_finalize(v, n=16, rms2=False, fold_recip_s=False):
            v = AutoV(v)
            v.tensor_scalar(TMPA[:, 0:n], STATB[:, 0:n], 1.0 / D, None, ALU.mult)
            v.tensor_scalar(TMPB[:, 0:n], STATA[:, 0:n], 1.0 / D, None, ALU.mult)
            v.tensor_tensor(TMPC[:, 0:n], TMPA[:, 0:n], TMPA[:, 0:n], ALU.mult)
            v.tensor_tensor(TMPB[:, 0:n], TMPB[:, 0:n], TMPC[:, 0:n], ALU.subtract)
            v.tensor_single_scalar(TMPC[:, 0:n], TMPB[:, 0:n], CONSC[:, 2:3], ALU.add)
            rsqrt(v, NSC[:, 0:n], TMPC[:, 0:n], TMPD[:, 0:n], TMPE[:, 0:n])
            if rms2:
                v.reciprocal(TMPD[:, 0:n], TMPC[:, 0:n])
                v.tensor_tensor(TMPD[:, 0:n], TMPD[:, 0:n], TMPB[:, 0:n], ALU.mult)
                v.tensor_single_scalar(TMPD[:, 0:n], TMPD[:, 0:n], CONSC[:, 3:4], ALU.add)
                rsqrt(v, TMPB[:, 0:n], TMPD[:, 0:n], TMPC[:, 0:n], TMPE[:, 0:n])
                v.tensor_tensor(NSC[:, 0:n], NSC[:, 0:n], TMPB[:, 0:n], ALU.mult)
            if fold_recip_s:
                v.tensor_single_scalar(NSC[:, 0:n], NSC[:, 0:n], SCL[:, 0:1], ALU.mult)
            v.scalar_tensor_tensor(NBI[:, 0:n], TMPA[:, 0:n], -1.0, NSC[:, 0:n],
                                   ALU.mult, ALU.mult)

        with ExitStack() as stA:
            EA = stA.enter_context
            KT = EA(nc.sbuf_tensor("KT", [128, 2 * TC], BF16))
            VT = EA(nc.sbuf_tensor("VT", [128, 2 * TC], BF16))
            QT = [EA(nc.sbuf_tensor(f"QT{h}", [128, TC], BF16)) for h in range(H)]
            with ExitStack() as stA1:
                EA1 = stA1.enter_context
                HTO = [EA1(nc.sbuf_tensor(f"HTO{c}", [128, TC], BF16)) for c in range(NCH)]
                XTI = [EA1(nc.sbuf_tensor(f"XTI{i}", [128, D], F32)) for i in range(2)]
                WQH = [EA1(nc.sbuf_tensor(f"WQH{i}", [128, D], BF16)) for i in range(2)]
                WKV = EA1(nc.sbuf_tensor("WKV", [128, NCH * 256], BF16))
                PTR = [EA1(nc.psum_tensor(f"PTR{i}", [128, 128], BF16)) for i in range(4)]
                PRJ = [EA1(nc.psum_tensor(f"PRJ{i}", [128, 512], F32)) for i in range(2)]
                PVV = [EA1(nc.psum_tensor(f"PVV{i}", [128, 128], F32)) for i in range(2)]

                @blk.gpsimd
                def _(g):
                    g.wait_ge(sS.h, p0_done)
                    sGD.inc(g.dma_start(
                        out=WKV[:, :], in_=wkv_t.transpose([1, 0, 2])), 16)
                    sWQ[0].inc(g.dma_start(
                        out=WQH[0][:, :],
                        in_=wq_t[0].transpose([1, 0, 2])), 16)
                g_wkv = sGD.n
                g_wq0 = sWQ[0].n

                for half in range(2):
                    x_src = xo_t if half == 0 else xr_t
                    for i in range(NT):
                        @blk.sync
                        def _(sy, i=i, x_src=x_src):
                            if i >= 1:
                                sy.wait_ge(sV.h, sV.n)
                            sXL[i % 2].inc(sy.dma_start(out=XTI[i % 2][:, :],
                                                        in_=x_src[i]), 16)
                        ld = sXL[i % 2].n

                        @blk.vector
                        def _(v, i=i, ld=ld, half=half):
                            if i == 0 and half == 0:
                                v.wait_ge(sS.h, p0_done)
                            v.wait_ge(sXL[i % 2].h, ld)
                            ln_stats_tile(v, XTI[i % 2][:, :], i)
                            sV.inc(v.engine_nop())

                    @blk.vector
                    def _(v):
                        ln_finalize(v, 16)
                        sV.inc(v.engine_nop())
                    fin = sV.n

                    for i in range(NT):
                        @blk.sync
                        def _(sy, i=i, x_src=x_src, fin=fin):
                            sy.wait_ge(sV.h, max(fin, sV.n if i >= 1 else 0))
                            sXL[i % 2].inc(sy.dma_start(out=XTI[i % 2][:, :],
                                                        in_=x_src[i]), 16)
                        ld = sXL[i % 2].n

                        @blk.vector
                        def _(v, i=i, ld=ld, fin=fin):
                            v.wait_ge(sXL[i % 2].h, ld)
                            if i >= 1:
                                v.wait_ge(sT.h, sT.n)
                            sV.inc(v.tensor_scalar(HBF[:, :], XTI[i % 2][:, :],
                                                   NSC[:, i:i+1], NBI[:, i:i+1],
                                                   ALU.mult, ALU.add))
                        nrm = sV.n

                        base_v = sV.n

                        @blk.tensor
                        def _(t, i=i, nrm=nrm, base_v=base_v):
                            t.wait_ge(sV.h, nrm)
                            for c in range(NCH):
                                if c >= 4:
                                    t.wait_ge(sV.h, base_v + c - 3)
                                sT.inc(t.transpose(PTR[c % 4][:, :],
                                                   HBF[:, c*128:(c+1)*128], IDENT[:, :]))
                        tr = sT.n

                        @blk.vector
                        def _(v, i=i, half=half, tr=tr):
                            for c in range(NCH):
                                v.wait_ge(sT.h, tr - 15 + c)
                                sV.inc(v.tensor_copy(HTO[c][:, i*128:(i+1)*128],
                                                     PTR[c % 4][:, :]))
                    htod = sV.n

                    kt_base = sV.n
                    for qs in range(4):
                        @blk.tensor
                        def _(t, qs=qs, htod=htod, kt_base=kt_base):
                            t.wait_ge(sV.h, max(htod, kt_base) if qs < 2 else sV.n)
                            t.wait_ge(sGD.h, g_wkv)
                            for c in range(NCH):
                                ins = t.matmul(PRJ[qs % 2][:, :], WKV[:, c*256:c*256+128],
                                               HTO[c][:, qs*512:(qs+1)*512],
                                               start=(c == 0), stop=(c == NCH - 1))
                            sT.inc(ins)
                        mmd = sT.n

                        @blk.vector
                        def _(v, qs=qs, half=half, mmd=mmd):
                            v.wait_ge(sT.h, mmd)
                            sV.inc(v.tensor_copy(
                                KT[:, half*2048 + qs*512: half*2048 + (qs+1)*512],
                                PRJ[qs % 2][:, :]))

                    vv_base = sV.n
                    for i in range(NT):
                        @blk.tensor
                        def _(t, i=i, vv_base=vv_base):
                            t.wait_ge(sV.h, sV.n if i >= 2 else vv_base)
                            for c in range(NCH):
                                ins = t.matmul(PVV[i % 2][:, :], HTO[c][:, i*128:(i+1)*128],
                                               WKV[:, c*256+128:c*256+256],
                                               start=(c == 0), stop=(c == NCH - 1))
                            sT.inc(ins)
                        mmd = sT.n

                        @blk.vector
                        def _(v, i=i, half=half, mmd=mmd):
                            v.wait_ge(sT.h, mmd)
                            sV.inc(v.tensor_copy(
                                VT[:, (half*16 + i)*128:(half*16 + i + 1)*128],
                                PVV[i % 2][:, :]))

                    if half == 0:
                        for h in range(H):
                            @blk.gpsimd
                            def _(g, h=h):
                                if h + 1 < H:
                                    if h >= 1:
                                        g.wait_ge(sT.h, sT.n)
                                    sWQ[(h+1) % 2].inc(g.dma_start(
                                        out=WQH[(h+1) % 2][:, :],
                                        in_=wq_t[h+1].transpose([1, 0, 2])), 16)
                            wq_ld = sWQ[h % 2].n

                            q_base = sV.n
                            for qs in range(4):
                                @blk.tensor
                                def _(t, h=h, qs=qs, wq_ld=wq_ld, q_base=q_base):
                                    if qs == 0:
                                        t.wait_ge(sWQ[h % 2].h, wq_ld)
                                    t.wait_ge(sV.h, sV.n if qs >= 2 else q_base)
                                    for c in range(NCH):
                                        ins = t.matmul(PRJ[qs % 2][:, :],
                                                       WQH[h % 2][:, c*128:(c+1)*128],
                                                       HTO[c][:, qs*512:(qs+1)*512],
                                                       start=(c == 0),
                                                       stop=(c == NCH - 1))
                                    sT.inc(ins)
                                mmd = sT.n

                                @blk.vector
                                def _(v, h=h, qs=qs, mmd=mmd):
                                    v.wait_ge(sT.h, mmd)
                                    sV.inc(v.tensor_copy(QT[h][:, qs*512:(qs+1)*512],
                                                         PRJ[qs % 2][:, :]))
            projs_v = sV.n

            sFC = Sem(st, nc, "sFCg")
            for sl in range(4):
                for piece in range(4):
                    c0 = piece * (NCH // 4)
                    dst = fw_bf[sl][:, c0*512:(c0 + NCH // 4)*512] \
                        .rearrange("p (c w) -> c p w", w=512)
                    srcp = f8_t[sl, c0:c0 + NCH // 4]

                    @blk.gpsimd
                    def _(g, dst=dst, srcp=srcp):
                        sFC.inc(g.dma_start(out=dst, in_=srcp), 16)
            fc_all = sFC.n

            with ExitStack() as stA2:
                EA2 = stA2.enter_context
                ET = [EA2(nc.sbuf_tensor(f"ET{i}", [128, 1024], BF16)) for i in range(4)]
                CTXH = EA2(nc.sbuf_tensor("CTXH", [128, TC], BF16))
                RECR = EA2(nc.sbuf_tensor("RECR", [1, 1024], F32))
                RSSB = EA2(nc.sbuf_tensor("RSSB", [1, 1024], F32))
                RSC2 = EA2(nc.sbuf_tensor("RSC2", [1, 1024], F32))
                CTXU = EA2(nc.sbuf_tensor("CTXU", [128, 1024], BF16))
                MST = EA2(nc.sbuf_tensor("MST", [1, 1], F32))
                ABSW = [EA2(nc.sbuf_tensor(f"ABSW{i}", [128, NCH * 384], F32))
                        for i in range(2)]

                with ExitStack() as stM:
                    PBM1 = stM.enter_context(nc.psum_tensor("PBM1", [128, 512], F32))
                    PBM2 = stM.enter_context(nc.psum_tensor("PBM2", [128, 1], F32))

                    @blk.tensor
                    def _(t):
                        t.wait_ge(sV.h, projs_v)
                        sT.inc(t.matmul(PBM1[:, :], KT[:, 0:128], QT[0][:, 0:512],
                                        start=True, stop=True))
                    m_mm = sT.n

                    @blk.vector
                    def _(v):
                        v.wait_ge(sT.h, m_mm)
                        sV.inc(v.reduce_max(MST[:, :], PBM1[0:1, 0:512], AX.X))
                    m_red = sV.n

                    @blk.tensor
                    def _(t):
                        t.wait_ge(sV.h, m_red)
                        sT.inc(t.matmul(PBM2[:, :], ONES1F[:, :], MST[:, :],
                                        start=True, stop=True))
                    m_bc = sT.n

                    @blk.vector
                    def _(v):
                        v.wait_ge(sT.h, m_bc)
                        sV.inc(v.tensor_scalar(SCL[:, 4:5], PBM2[:, :], -ASCALE,
                                               None, ALU.mult))
                    m_done = sV.n

                SCPS = EA2(nc.psum_tensor("SCPS", [128, 2048], F32))
                CTXPS = EA2(nc.psum_tensor("CTXPS", [128, 1024], F32))
                RSPS = EA2(nc.psum_tensor("RSPS", [1, 1024], F32))

                absr = [0] * (H + 8)
                prev_fin = m_done
                for h in range(H):
                    for qp in range(2):
                        us = (2 * qp, 2 * qp + 1)
                        base_t = sT.n
                        base_s = sS.n
                        prev_sp = sSP.n

                        @blk.tensor
                        def _(t, h=h, qp=qp, base_s=base_s, m_done=m_done,
                              prev_fin=prev_fin):
                            t.wait_ge(sV.h, m_done)
                            t.wait_ge(sS.h, base_s)
                            q0 = QT[h][:, (2*qp)*512:(2*qp+1)*512]
                            q1 = QT[h][:, (2*qp+1)*512:(2*qp+2)*512]
                            for kp in range(16):
                                if kp == 1:
                                    t.wait_ge(sV.h, prev_fin)
                                for s_id, q_ap in ((0, q0), (1, q1)):
                                    if kp >= 1:
                                        t.wait_ge(sS.h,
                                                  base_s + 2*(kp-1) + s_id + 1)
                                    pb = s_id * 1024
                                    t.matmul(SCPS[:, pb:pb+512],
                                             KT[:, (2*kp)*128:(2*kp+1)*128], q_ap,
                                             start=True, stop=True)
                                    sT.inc(t.matmul(
                                        SCPS[:, pb+512:pb+1024],
                                        KT[:, (2*kp+1)*128:(2*kp+2)*128], q_ap,
                                        start=True, stop=True))
                                if kp >= 1:
                                    for s_id in (0, 1):
                                        eb = ET[2*s_id + (kp-1) % 2]
                                        kc0 = 2*(kp-1)
                                        rsl = RSPS[0:1, s_id*512:(s_id+1)*512]
                                        csl = CTXPS[:, s_id*512:(s_id+1)*512]
                                        t.matmul(rsl, ONESB[:, :], eb[:, 0:512],
                                                 start=(kp == 1), stop=False)
                                        t.matmul(rsl, ONESB[:, :], eb[:, 512:1024],
                                                 start=False, stop=False)
                                        t.matmul(csl, VT[:, kc0*128:(kc0+1)*128],
                                                 eb[:, 0:512],
                                                 start=(kp == 1), stop=False)
                                        sT.inc(t.matmul(
                                            csl, VT[:, (kc0+1)*128:(kc0+2)*128],
                                            eb[:, 512:1024], start=False, stop=False))
                            for s_id in (0, 1):
                                t.wait_ge(sS.h, base_s + 31 + s_id)
                                eb = ET[2*s_id + 15 % 2]
                                rsl = RSPS[0:1, s_id*512:(s_id+1)*512]
                                csl = CTXPS[:, s_id*512:(s_id+1)*512]
                                t.matmul(rsl, ONESB[:, :], eb[:, 0:512],
                                         start=False, stop=False)
                                t.matmul(rsl, ONESB[:, :], eb[:, 512:1024],
                                         start=False, stop=True)
                                t.matmul(csl, VT[:, 30*128:31*128], eb[:, 0:512],
                                         start=False, stop=False)
                                sT.inc(t.matmul(csl, VT[:, 31*128:32*128],
                                                eb[:, 512:1024],
                                                start=False, stop=True))
                        inner_t = sT.n

                        @blk.scalar
                        def _(s, h=h, qp=qp, base_t=base_t, m_done=m_done):
                            s.wait_ge(sV.h, m_done)
                            for kp in range(16):
                                for s_id in (0, 1):
                                    tn = (s_id + 1) if kp == 0 else (4*kp - 1 + s_id)
                                    s.wait_ge(sT.h, base_t + tn)
                                    pb = s_id * 1024
                                    sS.inc(s.activation(
                                        ET[2*s_id + kp % 2][:, :],
                                        SCPS[:, pb:pb+1024], AF.Exp,
                                        bias=SCL[:, 4:5], scale=ASCALE))
                        inner_s = sS.n

                        @blk.vector
                        def _(v, h=h, qp=qp, inner_t=inner_t, prev_sp=prev_sp):
                            v.wait_ge(sT.h, inner_t)
                            if qp == 0:
                                v.wait_ge(sSP.h, prev_sp)
                            sV.inc(v.tensor_copy(RSSB[:, :], RSPS[0:1, :]))
                            sV.inc(v.tensor_copy(CTXU[:, :], CTXPS[:, :]))
                        ev_done = sV.n

                        @blk.vector
                        def _(v):
                            v.wait_ge(sV.h, ev_done)
                            sV.inc(v.reciprocal(RECR[:, 0:512], RSSB[:, 0:512]))
                        rec0 = sV.n

                        @blk.vector
                        def _(v):
                            sV.inc(v.reciprocal(RECR[:, 512:1024], RSSB[:, 512:1024]))
                        rec1 = sV.n

                        @blk.tensor
                        def _(t, rec0=rec0, rec1=rec1):
                            t.wait_ge(sV.h, rec0)
                            t.matmul(CTXPS[:, 0:512], ONES1F[:, :], RECR[:, 0:512],
                                     start=True, stop=True)
                            t.wait_ge(sV.h, rec1)
                            sT.inc(t.matmul(CTXPS[:, 512:1024], ONES1F[:, :],
                                            RECR[:, 512:1024], start=True, stop=True))
                        bc_done = sT.n

                        @blk.vector
                        def _(v, h=h, qp=qp, bc_done=bc_done):
                            v.wait_ge(sT.h, bc_done)
                            for s_id in (0, 1):
                                qs = 2*qp + s_id
                                sV.inc(v.tensor_tensor(
                                    CTXH[:, qs*512:(qs+1)*512],
                                    CTXU[:, s_id*512:(s_id+1)*512],
                                    CTXPS[:, s_id*512:(s_id+1)*512], ALU.mult))
                                v.wait_ge(sV.h, sV.n)
                        prev_fin = sV.n
                    ctx_done = sV.n

                    @blk.gpsimd
                    def _(g, h=h, ctx_done=ctx_done):
                        g.wait_ge(sV.h, ctx_done)
                        sSP.inc(g.dma_start(
                            out=ctx_h[:, h].transpose([1, 0, 2]),
                            in_=CTXH[:, :].rearrange("p (c n) -> p c n", n=128)), 16)

                    nb = 2 if h < 8 else 1
                    for _k in range(nb):
                        b = 2*h + _k if h < 8 else 8 + h
                        @blk.gpsimd
                        def _(g, b=b, m_done=m_done):
                            if b < 2:
                                g.wait_ge(sV.h, m_done)
                            else:
                                g.wait_ge(sV.h, absr[b - 2])
                            if b < H:
                                ap = qkv_t[b].transpose([1, 0, 2])
                                dst = ABSW[b % 2][:, :]
                            else:
                                sl, hf = (b - H) // 2, (b - H) % 2
                                ap = proj_t[sl].transpose([1, 0, 2])[
                                    :, :, hf*256:(hf+1)*256]
                                dst = ABSW[b % 2][:, 0:NCH*256]
                            sAB[b % 2].inc(g.dma_start(out=dst, in_=ap), 16)
                        ldb = sAB[b % 2].n

                        @blk.vector
                        def _(v, b=b, ldb=ldb):
                            v.wait_ge(sAB[b % 2].h, ldb)
                            if b < H:
                                sV.inc(v.tensor_reduce(ABSD[:, b:b+1],
                                                       ABSW[b % 2][:, :], AX.X,
                                                       ALU.add,
                                                       apply_absolute_value=True))
                            else:
                                sV.inc(v.tensor_reduce(ABSE[:, b-H:b-H+1],
                                                       ABSW[b % 2][:, 0:NCH*256],
                                                       AX.X, ALU.add,
                                                       apply_absolute_value=True))
                        absr[b] = sV.n
            attn_sp = sSP.n
            absr_all = sV.n

        stX = ExitStack()
        XRES = stX.enter_context(nc.sbuf_tensor("XRES", [128, NT * 2048], F32))

        with ExitStack() as stW:
            EW = stW.enter_context
            WOS = EW(nc.sbuf_tensor("WOS", [128, NCH * 512], BF16))
            BOF = EW(nc.sbuf_tensor("BOF", [128, 2048], F32))
            CTXC = [EW(nc.sbuf_tensor(f"CTXC{i}", [128, H * 128], BF16)) for i in range(2)]
            POW = [EW(nc.psum_tensor(f"POW{i}", [128, 512], F32)) for i in range(2)]

            @blk.sync
            def _(sy):
                sy.wait_ge(sV.h, absr_all)
                sy.wait_ge(sSP.h, attn_sp)
                sIN.inc(sy.dma_start(out=BOF[:, :], in_=bo), 16)
            bo_ld = sIN.n

            @blk.gpsimd
            def _(g):
                g.wait_ge(sSP.h, attn_sp)
                g.wait_ge(sV.h, absr_all)
                for i in range(NT):
                    sXP.inc(g.dma_start(out=XRES[:, i*2048:(i+1)*2048], in_=xo_t[i]),
                            16)
            x_pre = sXP.n

            @blk.vector
            def _(v, x_pre=x_pre):
                v.wait_ge(sIN.h, bo_ld)
                v.wait_ge(sXP.h, x_pre)
                for i in range(NT):
                    sV.inc(v.tensor_tensor(XRES[:, i*2048:(i+1)*2048],
                                           XRES[:, i*2048:(i+1)*2048],
                                           BOF[:, :], ALU.add))
                    v.wait_ge(sV.h, sV.n)
            bo_cast = sV.n

            for sl in range(4):
                @blk.gpsimd
                def _(g, sl=sl):
                    if sl > 0:
                        g.wait_ge(sT.h, sT.n)
                    else:
                        g.wait_ge(sV.h, absr_all)
                        g.wait_ge(sSP.h, attn_sp)
                    sWO.inc(g.dma_start(
                        out=WOS[:, :],
                        in_=wo_t[sl].transpose([1, 0, 2])), 16)
                wo_ld = sWO.n

                for c in range(NT):
                    @blk.sync
                    def _(sy, c=c, sl=sl):
                        if c >= 2 or sl > 0:
                            sy.wait_ge(sT.h, sT.n)
                        else:
                            sy.wait_ge(sV.h, absr_all)
                            sy.wait_ge(sSP.h, attn_sp)
                        sCX[c % 2].inc(sy.dma_start(
                            out=CTXC[c % 2][:, :],
                            in_=ctx_h[c].transpose([1, 0, 2])), 16)
                    ctx_ld = sCX[c % 2].n

                    @blk.tensor
                    def _(t, sl=sl, c=c, ctx_ld=ctx_ld, wo_ld=wo_ld, x_pre=x_pre,
                          bo_cast=bo_cast):
                        t.wait_ge(sCX[c % 2].h, ctx_ld)
                        t.wait_ge(sWO.h, wo_ld)
                        if c >= 2 or sl > 0:
                            t.wait_ge(sV.h, sV.n)
                        po = POW[c % 2]
                        for h in range(H):
                            ins = t.matmul(po[:, :], CTXC[c % 2][:, h*128:(h+1)*128],
                                           WOS[:, h*512:(h+1)*512],
                                           start=(h == 0), stop=(h == H - 1))
                        sT.inc(ins)
                    mmd = sT.n

                    @blk.vector
                    def _(v, sl=sl, c=c, mmd=mmd, bo_cast=bo_cast):
                        v.wait_ge(sT.h, mmd)
                        if sl == 0 and c == 0:
                            v.wait_ge(sV.h, bo_cast)
                        xs = XRES[:, c*2048 + sl*512: c*2048 + (sl+1)*512]
                        sV.inc(v.tensor_tensor(xs, xs, POW[c % 2][:, :], ALU.add))
        x1_v = sV.n


        ffn_id = [0]

        def emit_ffn(x_ready_v, XB, tile_hook=None):
            fid = ffn_id[0]
            ffn_id[0] += 1
            NG = 16
            NPT = NT // NG
            with ExitStack() as stF:
                EF = stF.enter_context
                XQT = [[EF(nc.sbuf_tensor(f"XQT{fid}_{b}_{c}", [128, 128], BF16))
                        for c in range(NCH)] for b in range(2)]
                FW = [EF(nc.sbuf_tensor(f"FW{fid}_{i}", [128, NCH * 512], BF16))
                      for i in range(2)]
                Q8A = EF(nc.sbuf_tensor(f"Q8A{fid}", [128, 2048], FP8))
                HIB = EF(nc.sbuf_tensor(f"HIB{fid}", [128, 2048], BF16))
                SCR8 = EF(nc.sbuf_tensor(f"SCR8{fid}", [128, 2048], U8))
                XQB = EF(nc.sbuf_tensor(f"XQB{fid}", [128, 2048], BF16))
                PTF = [EF(nc.psum_tensor(f"PTF{fid}_{i}", [128, 128], BF16))
                       for i in range(4)]
                PFW = [EF(nc.psum_tensor(f"PFW{fid}_{i}", [128, 512], F32))
                       for i in range(2)]
                Z2 = SCRAP
                sFW = [Sem(st, nc, f"sFW{fid}_{i}") for i in range(2)]
                sFWs2 = [Sem(st, nc, f"sFWsy{fid}_{i}") for i in range(2)]
                fwmm = {}
                ev_d = {}
                HFW = NCH // 2 * 512

                st_done = [0] * NT
                for i in range(NT):
                    @blk.vector
                    def _(v, i=i, x_ready_v=x_ready_v):
                        if i == 0:
                            v.wait_ge(sV.h, x_ready_v)
                        ln_stats_tile(v, XB[:, i*2048:(i+1)*2048], i)
                        sV.inc(v.engine_nop())
                    st_done[i] = sV.n

                @blk.vector
                def _(v):
                    ln_finalize(v, 16, rms2=False, fold_recip_s=True)
                    sV.inc(v.engine_nop())
                fin = sV.n

                def emit_qdq_tile(grp, ii):
                    if True:
                        i = grp * NPT + ii

                        @blk.vector
                        def _(v, i=i, ii=ii, grp=grp):
                            v.wait_ge(sV.h, max(fin, sV.n))
                            v.wait_ge(sT.h, sT.n)
                            v = AutoV(v)
                            v.tensor_scalar(Z2[:, :], XB[:, i*2048:(i+1)*2048],
                                            NSC[:, i:i+1], NBI[:, i:i+1],
                                            ALU.mult, ALU.add)
                            v.tensor_scalar(Z2[:, :], Z2[:, :], 448.0, -448.0,
                                            ALU.min, ALU.max)
                            v.tensor_copy(Q8A[:, :], Z2[:, :])
                            v.tensor_scalar(SCR8[:, :].bitcast(FP8), Z2[:, :], 0.5,
                                            None, ALU.mult)
                            v.tensor_scalar(HIB[:, :], SCR8[:, :].bitcast(FP8), 2.0,
                                            None, ALU.mult)
                            v.tensor_copy(XQB[:, :], Q8A[:, :])
                            v.tensor_scalar(SCR8[:, :], Z2[:, :], 240.0, None,
                                            ALU.is_gt)
                            v.copy_predicated(XQB[:, :], SCR8[:, :], HIB[:, :])
                            v.tensor_scalar(SCR8[:, :], Z2[:, :], -240.0, None,
                                            ALU.is_lt)
                            v.copy_predicated(XQB[:, :], SCR8[:, :], HIB[:, :])
                        xq_done = sV.n
                        base_v = sV.n

                        @blk.tensor
                        def _(t, ii=ii, xq_done=xq_done, base_v=base_v):
                            t.wait_ge(sV.h, xq_done)
                            for c in range(NCH):
                                if c >= 4:
                                    t.wait_ge(sV.h, base_v + c - 3)
                                sT.inc(t.transpose(PTF[c % 4][:, :],
                                                   XQB[:, c*128:(c+1)*128],
                                                   IDENT[:, :]))
                        tr = sT.n

                        @blk.vector
                        def _(v, ii=ii, grp=grp, tr=tr):
                            for c in range(NCH):
                                v.wait_ge(sT.h, tr - 15 + c)
                                sV.inc(v.tensor_copy(
                                    XQT[grp % 2][c][:, ii*128:(ii+1)*128],
                                    PTF[c % 4][:, :]))
                    return sV.n

                def emit_mm_group(grp, xq_grp):
                    for sl in range(4):
                        s = grp * 4 + sl
                        if grp + 1 < NG and sl < NPT:
                            emit_qdq_tile(grp + 1, sl)

                        @blk.gpsimd
                        def _(g, s=s, sl=sl, xq_grp=xq_grp):
                            if s <= 1:
                                if s == 0 and fid == 0:
                                    g.wait_ge(sFC.h, fc_all)
                                g.wait_ge(sV.h, xq_grp)
                                g.wait_ge(sT.h, sT.n)
                                g.wait_ge(sS.h, sS.n)
                            else:
                                g.wait_ge(sT.h, fwmm[s - 2])
                            sFW[s % 2].inc(g.dma_start(
                                out=FW[s % 2][:, 0:HFW],
                                in_=fw_bf[sl][:, 0:HFW]), 16)
                        f8_ld = sFW[s % 2].n

                        @blk.sync
                        def _(sy, s=s, sl=sl, xq_grp=xq_grp):
                            if s <= 1:
                                if s == 0 and fid == 0:
                                    sy.wait_ge(sFC.h, fc_all)
                                sy.wait_ge(sV.h, xq_grp)
                                sy.wait_ge(sT.h, sT.n)
                                sy.wait_ge(sS.h, sS.n)
                            else:
                                sy.wait_ge(sT.h, fwmm[s - 2])
                            sFWs2[s % 2].inc(sy.dma_start(
                                out=FW[s % 2][:, HFW:2 * HFW],
                                in_=fw_bf[sl][:, HFW:2 * HFW]), 16)
                        f8_lds = sFWs2[s % 2].n

                        for ii in range(NPT):
                            i = grp * NPT + ii

                            @blk.tensor
                            def _(t, ii=ii, s=s, sl=sl, grp=grp, f8_ld=f8_ld,
                                  f8_lds=f8_lds, xq_grp=xq_grp):
                                t.wait_ge(sFW[s % 2].h, f8_ld)
                                t.wait_ge(sFWs2[s % 2].h, f8_lds)
                                t.wait_ge(sV.h, xq_grp)
                                if s >= 2:
                                    t.wait_ge(sV.h, ev_d[s - 2])
                                for c in range(NCH):
                                    ins = t.matmul(PFW[s % 2][:, :],
                                                   XQT[grp % 2][c][:, ii*128:(ii+1)*128],
                                                   FW[s % 2][:, c*512:(c+1)*512],
                                                   start=(c == 0),
                                                   stop=(c == NCH - 1))
                                sT.inc(ins)
                            mmd = sT.n
                            fwmm[s] = sT.n

                            @blk.vector
                            def _(v, i=i, s=s, sl=sl, mmd=mmd):
                                v.wait_ge(sT.h, mmd)
                                xs = XB[:, i*2048 + sl*512: i*2048 + (sl+1)*512]
                                sV.inc(v.scalar_tensor_tensor(
                                    xs, PFW[s % 2][:, :], SCL[:, 1:2], xs,
                                    ALU.mult, ALU.add))
                            ev_d[s] = sV.n
                            if sl == 3 and tile_hook is not None:
                                tile_hook(i)

                for ii in range(NPT):
                    emit_qdq_tile(0, ii)
                xq = sV.n
                for grp in range(NG):
                    emit_mm_group(grp, xq)
                    xq = sV.n
            return sV.n

        if stage == 1:
            @blk.sync
            def _(sy):
                sy.wait_ge(sV.h, x1_v)
                for i in range(NT):
                    sOUT.inc(sy.dma_start(out=out_t[i],
                                          in_=XRES[:, i*2048:(i+1)*2048]), 16)
                sy.wait_ge(sOUT.h, sOUT.n)
            stX.close()
            return nc

        c1_done = [0]

        def _c1_hook(i):
            @blk.vector
            def _(v, i=i):
                v.wait_ge(sV.h, sV.n)
                ln_stats_tile(v, XRES[:, i*2048:(i+1)*2048], i)
                sV.inc(v.engine_nop())
            c1_done[0] = sV.n

        x2_v = emit_ffn(x1_v, XRES, tile_hook=_c1_hook)

        if stage == 2:
            @blk.sync
            def _(sy):
                sy.wait_ge(sV.h, x2_v)
                for i in range(NT):
                    sOUT.inc(sy.dma_start(out=out_t[i],
                                          in_=XRES[:, i*2048:(i+1)*2048]), 16)
                sy.wait_ge(sOUT.h, sOUT.n)
            stX.close()
            return nc

        sXS = Sem(st, nc, "sXS")

        @blk.sync
        def _(sy):
            sy.wait_ge(sV.h, x2_v)
            for i in range(NT):
                sXS.inc(sy.dma_start(out=x2_h[i], in_=XRES[:, i*2048:(i+1)*2048]), 16)
        x2sp = sXS.n
        stX.close()

        x3_dst = out_t if stage == 3 else x3_h

        stCC = ExitStack()
        SHALF = stCC.enter_context(nc.sbuf_tensor("SHALF", [128, H * 128], F32))
        SINIT = stCC.enter_context(nc.sbuf_tensor("SINIT", [128, H * 128], F32))
        RS = stCC.enter_context(nc.sbuf_tensor("RS", [128, NT], F32))
        SCLQ = stCC.enter_context(nc.sbuf_tensor("SCLQ", [128, 1], F32))
        MQH = stCC.enter_context(nc.sbuf_tensor("MQH", [128, 1], F32))
        SCLP = stCC.enter_context(nc.sbuf_tensor("SCLP", [128, 1], F32))
        MPH = stCC.enter_context(nc.sbuf_tensor("MPH", [128, 1], F32))
        NEGH = stCC.enter_context(nc.sbuf_tensor("NEGH", [128, 1], F32))
        POSH = stCC.enter_context(nc.sbuf_tensor("POSH", [128, 1], F32))
        ONESF = stCC.enter_context(nc.sbuf_tensor("ONESF", [128, 1], F32))
        sQW = [Sem(st, nc, f"sQW{i}") for i in range(2)]
        sQS = Sem(st, nc, "sQS")
        sQWs = [Sem(st, nc, f"sQWs{i}") for i in range(2)]
        sQT = Sem(st, nc, "sQT")
        sQL = [Sem(st, nc, f"sQL{i}") for i in range(4)]
        sX2 = [Sem(st, nc, f"sX2{i}") for i in range(2)]
        sPW = Sem(st, nc, "sPW")
        sX3 = Sem(st, nc, "sX3")

        @blk.gpsimd
        def _(g):
            g.wait_ge(sXS.h, x2sp)
            g.memset(NEGH[:, :], -0.5)
            g.memset(POSH[:, :], 0.5)
            sG.inc(g.memset(ONESF[:, :], 1.0))
        consts_g = sG.n

        with ExitStack() as stC:
            EC = stC.enter_context
            QWF = [EC(nc.sbuf_tensor("QWF0", [128, NCH * 384], F32))] * 2

            stB = ExitStack()
            PBC = stB.enter_context(nc.psum_tensor("PBC2", [128, 512], F32))
            @blk.vector
            def _(v):
                v.wait_ge(sV.h, max(absr_all, sV.n))
                v.wait_ge(sG.h, consts_g)
                v = AutoV(v)
                v.tensor_reduce(TMPC[:, 0:1], ABSD[:, 0:16], AX.X, ALU.add)
                v.tensor_reduce(TMPC[:, 1:2], ABSE[:, 0:8], AX.X, ALU.add)
            abs_v = sV.n

            @blk.tensor
            def _(t):
                t.wait_ge(sV.h, abs_v)
                sT.inc(t.matmul(PBC[0:1, 0:2], ONESF[:, :], TMPC[:, 0:2],
                                start=True, stop=True))
            abs_t = sT.n

            @blk.vector
            def _(v):
                v.wait_ge(sT.h, abs_t)
                v = AutoV(v)
                v.tensor_scalar(TMPC[0:1, 2:3], PBC[0:1, 0:1],
                                1.0 / (6144.0 * 2048.0), 1e-5, ALU.mult, ALU.max)
                v.tensor_scalar(TMPC[0:1, 3:4], PBC[0:1, 1:2],
                                1.0 / (2048.0 * 2048.0), 1e-5, ALU.mult, ALU.max)
                v.reciprocal(TMPC[0:1, 4:5], TMPC[0:1, 2:3])
                v.reciprocal(TMPC[0:1, 5:6], TMPC[0:1, 3:4])
            mq_v = sV.n

            @blk.tensor
            def _(t):
                t.wait_ge(sV.h, mq_v)
                sT.inc(t.matmul(PBC[:, 0:4], ONES1F[:, :], TMPC[0:1, 2:6],
                                start=True, stop=True))
            mq_t = sT.n

            @blk.vector
            def _(v):
                v.wait_ge(sT.h, mq_t)
                v = AutoV(v)
                v.tensor_scalar(MQH[:, :], PBC[:, 0:1], 0.5, None, ALU.mult)
                v.tensor_scalar(MPH[:, :], PBC[:, 1:2], 0.5, None, ALU.mult)
                v.tensor_copy(SCLQ[:, :], PBC[:, 2:3])
                v.tensor_copy(SCLP[:, :], PBC[:, 3:4])
            mrep_v = sV.n

            stB.close()

            with ExitStack() as stG:
                EG = stG.enter_context
                GT = [EG(nc.sbuf_tensor(f"GT{c}", [128, TC], BF16)) for c in range(NCH)]
                PTC = [EG(nc.psum_tensor(f"PTC{i}", [128, 128], BF16)) for i in range(2)]

                with ExitStack() as stT2:
                    XT2 = [stT2.enter_context(nc.sbuf_tensor(f"XT2{i}", [128, D], F32))
                           for i in range(2)]
                    @blk.vector
                    def _(v):
                        v.wait_ge(sV.h, max(c1_done[0], mrep_v))
                        ln_finalize(v, 16, rms2=True)
                        sV.inc(v.engine_nop())
                    fin = sV.n

                    for i in range(NT):
                        @blk.sync
                        def _(sy, i=i, fin=fin):
                            if i == 0:
                                sy.wait_ge(sXS.h, x2sp)
                            sy.wait_ge(sV.h, max(fin, sV.n if i >= 1 else 0))
                            sX2[i % 2].inc(sy.dma_start(out=XT2[i % 2][:, :],
                                                        in_=x2_h[i]), 16)
                        ld = sX2[i % 2].n

                        @blk.vector
                        def _(v, i=i, ld=ld, fin=fin):
                            v.wait_ge(sX2[i % 2].h, ld)
                            if i >= 1:
                                v.wait_ge(sT.h, sT.n)
                            sV.inc(v.tensor_scalar(HBF[:, :], XT2[i % 2][:, :],
                                                   NSC[:, i:i+1], NBI[:, i:i+1],
                                                   ALU.mult, ALU.add))
                        nrm = sV.n
                        base_v = sV.n

                        @blk.tensor
                        def _(t, i=i, nrm=nrm, base_v=base_v):
                            t.wait_ge(sV.h, nrm)
                            for c in range(NCH):
                                if c >= 2:
                                    t.wait_ge(sV.h, base_v + c - 1)
                                sT.inc(t.transpose(PTC[c % 2][:, :],
                                                   HBF[:, c*128:(c+1)*128], IDENT[:, :]))
                        tr = sT.n

                        @blk.vector
                        def _(v, i=i, tr=tr):
                            for c in range(NCH):
                                v.wait_ge(sT.h, tr - 15 + c)
                                sV.inc(v.tensor_single_scalar(
                                    GT[c][:, i*128:(i+1)*128], PTC[c % 2][:, :],
                                    QNW[:, c:c+1], ALU.mult))
                gt_v = sV.n

                stC3 = ExitStack()
                E3 = stC3.enter_context
                QS1 = E3(nc.sbuf_tensor("QS1", [128, NCH * 384], BF16))
                QTRN2 = [E3(nc.sbuf_tensor(f"QTRN{i}", [128, NCH * 384], BF16))
                         for i in range(2)]
                QTH = E3(nc.sbuf_tensor("QTH", [128, TC], BF16))
                KTH = E3(nc.sbuf_tensor("KTH", [128, TC], BF16))
                KTOK = E3(nc.sbuf_tensor("KTOK", [128, TC], BF16))
                VTH = E3(nc.sbuf_tensor("VTH", [128, TC], BF16))
                PRJ3 = [E3(nc.psum_tensor(f"PRJ3{i}", [128, 512], F32)) for i in range(2)]
                PV3 = [E3(nc.psum_tensor(f"PV3{i}", [128, 128], F32)) for i in range(2)]
                PK3 = E3(nc.psum_tensor("PK3", [128, 128], F32))
                KLAMF3 = E3(nc.sbuf_tensor("KLAMF3", [128, TC], BF16))
                prev_qt = [0]
                prev_vproj = {0: 0, 1: 0}
                kv_t = {}
                shcp = {}
                klam_vd = {}

                def emit_kv(h):
                    @blk.tensor
                    def _(t, h=h):
                        t.wait_ge(sV.h, klam_vd[h])
                        if h >= 1:
                            t.wait_ge(sV.h, shcp[h - 1])
                        for j in range(NJ):
                            sT.inc(t.matmul(PK3[:, :],
                                            KLAMF3[:, j*128:(j+1)*128],
                                            VTH[:, j*128:(j+1)*128],
                                            start=(j == 0), stop=(j == NJ - 1)))
                    kv_t[h] = sT.n

                    @blk.vector
                    def _(v, h=h):
                        v.wait_ge(sT.h, kv_t[h])
                        sV.inc(v.tensor_copy(SHALF[:, h*128:(h+1)*128], PK3[:, :]))
                    shcp[h] = sV.n
                qk_sp = {}
                vk_sp = {}
                HHALF = NCH // 2 * 384
                for h in range(H):
                    par = h % 2

                    @blk.gpsimd
                    def _(g, h=h, gt_v=gt_v):
                        if h == 0:
                            g.wait_ge(sV.h, gt_v)
                            g.wait_ge(sXS.h, x2sp)
                        if h >= 1:
                            g.wait_ge(sS.h, sS.n)
                        sQW[h % 2].inc(g.dma_start(
                            out=QWF[h % 2][:, :],
                            in_=qkv_t[h].transpose([1, 0, 2])), 16)
                    qw_ld = sQW[h % 2].n

                    @blk.scalar
                    def _(s, h=h, par=par, qw_ld=qw_ld):
                        s.wait_ge(sQW[h % 2].h, qw_ld)
                        if h >= 2:
                            s.wait_ge(sT.h, prev_vproj[par])
                        else:
                            s.wait_ge(sT.h, sT.n)
                        if h >= 1:
                            s.wait_ge(sV.h, prev_qt[0])
                        else:
                            s.wait_ge(sV.h, sV.n)
                        s.activation(QTRN2[par][:, :], QWF[h % 2][:, :], AF.Sign,
                                     bias=NEGH[:, 0:1], scale=SCLQ[:, 0:1])
                        sS.inc(s.activation(QS1[:, :], QWF[h % 2][:, :], AF.Sign,
                                            bias=POSH[:, 0:1], scale=SCLQ[:, 0:1]))
                    sgn = sS.n

                    @blk.vector
                    def _(v, par=par, sgn=sgn):
                        v.wait_ge(sS.h, sgn)
                        sV.inc(v.tensor_tensor(QTRN2[par][:, :], QTRN2[par][:, :],
                                               QS1[:, :], ALU.add))
                    qt_v = sV.n
                    prev_qt[0] = qt_v

                    if h >= 1:
                        emit_kv(h - 1)

                    for oi, DST in ((0, QTH), (1, KTH)):
                        base_ev = sV.n
                        for qs in range(4):
                            @blk.tensor
                            def _(t, par=par, oi=oi, qs=qs, qt_v=qt_v,
                                  base_ev=base_ev):
                                t.wait_ge(sV.h, max(qt_v, base_ev if qs < 2 else sV.n))
                                for c in range(NCH):
                                    ins = t.matmul(
                                        PRJ3[qs % 2][:, :],
                                        QTRN2[par][:, c*384+oi*128:c*384+(oi+1)*128],
                                        GT[c][:, qs*512:(qs+1)*512],
                                        start=(c == 0), stop=(c == NCH - 1))
                                sT.inc(ins)
                            mmd = sT.n

                            @blk.vector
                            def _(v, h=h, oi=oi, qs=qs, DST=DST, mmd=mmd):
                                v.wait_ge(sT.h, mmd)
                                if h >= 1 and oi == 0 and qs == 0:
                                    v.wait_ge(sQT.h, vk_sp[h - 1])
                                sV.inc(v.tensor_single_scalar(
                                    DST[:, qs*512:(qs+1)*512], PRJ3[qs % 2][:, :],
                                    MQH[:, 0:1], ALU.mult))
                        ev_d = sV.n

                        @blk.sync
                        def _(sy, h=h, oi=oi, DST=DST, ev_d=ev_d):
                            sy.wait_ge(sV.h, ev_d)
                            sQT.inc(sy.dma_start(out=qkv_h[h, oi], in_=DST[:, :]),
                                    16)
                        if oi == 1:
                            qk_sp[h] = sQT.n

                    vbase = sV.n
                    for i in range(NT):
                        @blk.tensor
                        def _(t, par=par, i=i, qt_v=qt_v, vbase=vbase):
                            t.wait_ge(sV.h, max(qt_v, vbase if i < 2 else sV.n))
                            for c in range(NCH):
                                ins = t.matmul(PV3[i % 2][:, :],
                                               GT[c][:, i*128:(i+1)*128],
                                               QTRN2[par][:, c*384+256:c*384+384],
                                               start=(c == 0), stop=(c == NCH - 1))
                            sT.inc(ins)
                        mmd = sT.n

                        @blk.vector
                        def _(v, i=i, mmd=mmd):
                            v.wait_ge(sT.h, mmd)
                            sV.inc(v.tensor_single_scalar(
                                VTH[:, i*128:(i+1)*128], PV3[i % 2][:, :],
                                MQH[:, 0:1], ALU.mult))
                    vth_v = sV.n
                    prev_vproj[par] = sT.n

                    @blk.sync
                    def _(sy, h=h, vth_v=vth_v):
                        sy.wait_ge(sV.h, vth_v)
                        sQT.inc(sy.dma_start(out=qkv_h[h, 3], in_=VTH[:, :]), 16)

                    tbase = sV.n
                    for c in range(NT):
                        @blk.tensor
                        def _(t, c=c, vth_v=vth_v, tbase=tbase):
                            t.wait_ge(sV.h, max(vth_v, tbase if c < 2 else sV.n))
                            sT.inc(t.transpose(PTC[c % 2][:, :],
                                               KTH[:, c*128:(c+1)*128], IDENT[:, :]))
                        trd = sT.n

                        @blk.vector
                        def _(v, c=c, trd=trd):
                            v.wait_ge(sT.h, trd)
                            sV.inc(v.tensor_copy(KTOK[:, c*128:(c+1)*128],
                                                 PTC[c % 2][:, :]))
                    ktok_v = sV.n

                    @blk.sync
                    def _(sy, h=h, ktok_v=ktok_v):
                        sy.wait_ge(sV.h, ktok_v)
                        sQT.inc(sy.dma_start(out=qkv_h[h, 2], in_=KTOK[:, :]), 16)
                    vk_sp[h] = sQT.n

                    @blk.vector
                    def _(v, h=h, ktok_v=ktok_v):
                        v.wait_ge(sV.h, max(ktok_v, sV.n))
                        if h >= 1:
                            v.wait_ge(sT.h, kv_t[h - 1])
                        kl3 = KLAMF3[:, :].rearrange("p (j d) -> p j d", d=128)
                        kt3 = KTOK[:, :].rearrange("p (j d) -> p j d", d=128)
                        lj_b = LAMJ[:, h*NJ:(h+1)*NJ].unsqueeze(2) \
                            .broadcast_to([128, NJ, 128])
                        sV.inc(v.scalar_tensor_tensor(
                            kl3, kt3, LAMRV[:, h:h+1], lj_b,
                            ALU.mult, ALU.mult))
                    klam_vd[h] = sV.n
                emit_kv(H - 1)
                c3_sp = sQT.n
                c3_v = sV.n
                stC3.close()

            @blk.gpsimd
            def _(g, c3_sp=c3_sp, c3_v=c3_v):
                g.wait_ge(sQT.h, c3_sp)
                g.wait_ge(sV.h, c3_v)
                sQS.inc(g.dma_start(out=cc_in.ap(), in_=SHALF[:, :]), 16)
                g.wait_ge(sQS.h, sQS.n)
                g.collective_compute(
                    "AllGather", ALU.bypass,
                    replica_groups=[[0, 1], [2, 3], [4, 5], [6, 7]],
                    ins=[cc_in.ap().opt()],
                    outs=[cc_out.ap().opt()],
                ).then_inc(sCC.h, 1)
                sCC.n += 1
                g.wait_ge(sCC.h, sCC.n)
                sQS.inc(g.dma_start(out=SINIT[:, :], in_=cc_out[0]), 16)
            cc_g = sQS.n

            @blk.vector
            def _(v, cc_g=cc_g):
                v.wait_ge(sQS.h, cc_g)
                sV.inc(v.tensor_single_scalar(SINIT[:, :], SINIT[:, :], SMK[:, 0:1],
                                              ALU.mult))
            s0_v = sV.n

        STR = stCC.enter_context(nc.sbuf_tensor("STR", [2, 2048], F32))
        with ExitStack() as stS:
            ES = stS.enter_context
            OALL = [ES(nc.sbuf_tensor(f"OALL{h}", [128, TC], BF16)) for h in range(H)]
            stS2 = ExitStack()
            ES2 = stS2.enter_context
            NB = 4
            QKVL = [ES2(nc.sbuf_tensor(f"QKVL{i}", [128, 4 * TC], BF16))
                    for i in range(NB)]
            SB16 = [ES2(nc.sbuf_tensor(f"SB16_{i}", [128, 128], BF16))
                    for i in range(2)]
            E2 = [ES2(nc.sbuf_tensor(f"E2_{i}", [128, 128], BF16)) for i in range(2)]
            QLAMF = [ES2(nc.sbuf_tensor(f"QLAMF{i}", [128, TC], BF16))
                     for i in range(2)]
            KLAMF = [ES2(nc.sbuf_tensor(f"KLAMF{i}", [128, TC], BF16))
                     for i in range(2)]
            OSQ = ES2(nc.sbuf_tensor("OSQ", [128, TC], BF16))
            SACC = ES2(nc.sbuf_tensor("SACC", [2, 2048], F32))
            with ExitStack() as stP:
                PSC = [stP.enter_context(nc.psum_tensor(f"PSC{i}", [128, 128], F32))
                       for i in range(2)]
                PO5 = [stP.enter_context(nc.psum_tensor(f"PO5{i}", [128, 128], F32))
                       for i in range(2)]
                PK5 = [stP.enter_context(nc.psum_tensor(f"PK5{i}", [128, 128], F32))
                       for i in range(2)]
                PSTQ = stP.enter_context(nc.psum_tensor("PSTQ", [2, 1024], F32))
                c5_t0 = sT.n
                c5_s0 = sS.n

                @blk.vector
                def _(v):
                    v.wait_ge(sT.h, sT.n)
                    v.wait_ge(sS.h, sS.n)
                    sV.inc(v.memset(SACC[:, :], 0.0))
                sacc0_v = sV.n

                qld = {}
                init_v = {}
                psct = {}
                pkt = {}
                pot = {}
                e2v = {}
                sttv = {}
                oevs = {}
                osq_v = {}
                stats_t = {}
                pstq_free_v = 0

                def emit_load(h):
                    @blk.sync
                    def _(sy, h=h):
                        if h == 0:
                            sy.wait_ge(sQT.h, c3_sp)
                            sy.wait_ge(sV.h, c3_v)
                            sy.wait_ge(sT.h, c5_t0)
                            sy.wait_ge(sS.h, c5_s0)
                            sy.wait_ge(sQW[0].h, sQW[0].n)
                            sy.wait_ge(sQW[1].h, sQW[1].n)
                        elif h >= NB:
                            sy.wait_ge(sT.h, pot[(h - NB, NJ - 1)])
                        sQL[h % NB].inc(sy.dma_start(
                            out=QKVL[h % NB][:, :],
                            in_=qkv_h[h].transpose([1, 0, 2])), 16)
                    qld[h] = sQL[h % NB].n

                def emit_init(h):
                    par = h % 2
                    qb = QKVL[h % NB]
                    Q5 = qb[:, 0:2048]
                    KT5 = qb[:, 4096:6144]

                    @blk.vector
                    def _(v, h=h, par=par, Q5=Q5, KT5=KT5):
                        v.wait_ge(sQL[h % NB].h, qld[h])
                        if h >= 2:
                            v.wait_ge(sT.h, pot[(h - 2, NJ - 1)])
                            v.wait_ge(sV.h, sttv[(h - 2, NJ - 1)])
                        else:
                            v.wait_ge(sT.h, c5_t0)
                            v.wait_ge(sS.h, c5_s0)
                            v.wait_ge(sV.h, s0_v)
                        sV.inc(v.tensor_copy(SB16[par][:, :],
                                             SINIT[:, h*128:(h+1)*128]))
                        lam_b = LAMQ[:, h*128:(h+1)*128].unsqueeze(1) \
                            .broadcast_to([128, NJ, 128])
                        q3 = Q5.rearrange("p (j t) -> p j t", t=128)
                        qo = QLAMF[par][:, :].rearrange("p (j t) -> p j t", t=128)
                        sV.inc(v.tensor_tensor(qo, q3, lam_b, ALU.mult))
                        sV.inc(v.tensor_single_scalar(KLAMF[par][:, :], KT5,
                                                      LAMRV[:, h:h+1], ALU.mult))
                    init_v[h] = sV.n

                mm_t = {}
                adds_v = {}

                def emit_osq(h):
                    @blk.vector
                    def _(v, h=h):
                        v.wait_ge(sS.h, oevs[(h, NJ - 1)])
                        if h >= 1:
                            v.wait_ge(sT.h, stats_t[h - 1])
                        else:
                            v.wait_ge(sT.h, c5_t0)
                        sV.inc(v.tensor_tensor(OSQ[:, :], OALL[h][:, :],
                                               OALL[h][:, :], ALU.mult))
                    osq_v[h] = sV.n

                def emit_stats_mm(h, r):
                    @blk.tensor
                    def _(t, h=h, r=r):
                        if r == 0:
                            t.wait_ge(sV.h, osq_v[h])
                            if h >= 1:
                                t.wait_ge(sV.h, adds_v[(h - 1, 1)])
                        else:
                            t.wait_ge(sV.h, adds_v[(h, 0)])
                        for half in range(2):
                            qs = 2 * r + half
                            sT.inc(t.matmul(
                                PSTQ[:, half*512:half*512 + 512],
                                W1SQ[:, 2*h:2*h+2],
                                OSQ[:, qs*512:(qs+1)*512],
                                start=True, stop=True))
                    mm_t[(h, r)] = sT.n
                    if r == 1:
                        stats_t[h] = sT.n

                def emit_adds(h, r):
                    nonlocal pstq_free_v

                    @blk.vector
                    def _(v, h=h, r=r):
                        v.wait_ge(sT.h, mm_t[(h, r)])
                        if h == 0 and r == 0:
                            v.wait_ge(sV.h, sacc0_v)
                        for half in range(2):
                            qs = 2 * r + half
                            sV.inc(v.tensor_tensor(
                                SACC[:, qs*512:(qs+1)*512],
                                SACC[:, qs*512:(qs+1)*512],
                                PSTQ[:, half*512:half*512 + 512],
                                ALU.add))
                    adds_v[(h, r)] = sV.n
                    pstq_free_v = sV.n

                def emit_scan_j(h, j):
                    par = h % 2
                    qb = QKVL[h % NB]
                    Q5 = qb[:, 0:2048]
                    K5 = qb[:, 2048:4096]
                    V5 = qb[:, 6144:8192]

                    @blk.tensor
                    def _(t, h=h, par=par, j=j, Q5=Q5, K5=K5):
                        if j == 0:
                            t.wait_ge(sV.h, init_v[h])
                        else:
                            t.wait_ge(sV.h, e2v[(h, j-1)])
                        sT.inc(t.matmul(PSC[par][:, :],
                                        K5[:, j*128:(j+1)*128],
                                        Q5[:, j*128:(j+1)*128],
                                        start=True, stop=True))
                    psct[(h, j)] = sT.n

                    @blk.tensor
                    def _(t, h=h, par=par, j=j, V5=V5):
                        if j >= 1:
                            t.wait_ge(sV.h, sttv[(h, j-1)])
                        sT.inc(t.matmul(PK5[par][:, :],
                                        KLAMF[par][:, j*128:(j+1)*128],
                                        V5[:, j*128:(j+1)*128],
                                        start=True, stop=True))
                    pkt[(h, j)] = sT.n

                    @blk.vector
                    def _(v, h=h, par=par, j=j):
                        v.wait_ge(sT.h, psct[(h, j)])
                        sV.inc(v.tensor_tensor(E2[par][:, :], PSC[par][:, :],
                                               MASKT[:, h*128:(h+1)*128],
                                               ALU.mult))
                    e2v[(h, j)] = sV.n

                    @blk.tensor
                    def _(t, h=h, par=par, j=j, V5=V5):
                        t.wait_ge(sV.h, e2v[(h, j)])
                        if j >= 1:
                            t.wait_ge(sV.h, sttv[(h, j-1)])
                            t.wait_ge(sS.h, oevs[(h, j-1)])
                        elif h >= 2:
                            t.wait_ge(sS.h, oevs[(h - 2, NJ - 1)])
                        t.matmul(PO5[par][:, :], V5[:, j*128:(j+1)*128],
                                 E2[par][:, :], start=True, stop=False)
                        sT.inc(t.matmul(PO5[par][:, :], SB16[par][:, :],
                                        QLAMF[par][:, j*128:(j+1)*128],
                                        start=False, stop=True))
                    pot[(h, j)] = sT.n

                    @blk.vector
                    def _(v, h=h, par=par, j=j):
                        v.wait_ge(sT.h, pot[(h, j)])
                        v.wait_ge(sT.h, pkt[(h, j)])
                        sV.inc(v.scalar_tensor_tensor(
                            SB16[par][:, :], SB16[par][:, :],
                            LAMTT[:, h:h+1], PK5[par][:, :],
                            ALU.mult, ALU.add))
                    sttv[(h, j)] = sV.n

                    @blk.scalar
                    def _(s, h=h, par=par, j=j):
                        s.wait_ge(sT.h, pot[(h, j)])
                        sS.inc(s.activation(OALL[h][:, j*128:(j+1)*128],
                                            PO5[par][:, :], AF.Copy,
                                            scale=OSCALE))
                    oevs[(h, j)] = sS.n

                for h in range(NB):
                    emit_load(h)
                emit_init(0)
                emit_init(1)

                for hp in range(H // 2):
                    hA, hB = 2 * hp, 2 * hp + 1
                    pA, pB = hA - 2, hB - 2
                    for j in range(NJ):
                        if hp >= 1:
                            if j == 2:
                                emit_osq(pA)
                                emit_stats_mm(pA, 0)
                            elif j == 4:
                                emit_adds(pA, 0)
                                emit_stats_mm(pA, 1)
                            elif j == 6:
                                emit_adds(pA, 1)
                                emit_osq(pB)
                                emit_stats_mm(pB, 0)
                            elif j == 8:
                                emit_adds(pB, 0)
                                emit_stats_mm(pB, 1)
                            elif j == 10:
                                emit_adds(pB, 1)
                        emit_scan_j(hA, j)
                        emit_scan_j(hB, j)
                    for h in (hA + NB, hB + NB):
                        if h < H:
                            emit_load(h)
                    for h in (hA + 2, hB + 2):
                        if h < H:
                            emit_init(h)
                for h in (H - 2, H - 1):
                    emit_osq(h)
                    emit_stats_mm(h, 0)
                    emit_adds(h, 0)
                    emit_stats_mm(h, 1)
                    emit_adds(h, 1)

                @blk.vector
                def _(v):
                    v.wait_ge(sV.h, pstq_free_v)
                    sV.inc(v.tensor_copy(STR[:, :], SACC[:, :]))
                str_v = sV.n
            stS2.close()

            with ExitStack() as stP2:
                PTS = stP2.enter_context(nc.psum_tensor("PTS", [128, 2], F32))
                for c in range(NT):
                    @blk.tensor
                    def _(t, c=c, str_v=str_v):
                        t.wait_ge(sV.h, str_v if c == 0 else sV.n)
                        sT.inc(t.transpose(PTS[:, :], STR[0:2, c*128:(c+1)*128],
                                           IDENTF[0:2, 0:2]))
                    pt_t = sT.n

                    @blk.vector
                    def _(v, c=c, pt_t=pt_t):
                        v.wait_ge(sT.h, pt_t)
                        sV.inc(v.tensor_copy(TMPD[:, c:c+1], PTS[:, 0:1]))
                        v.wait_ge(sV.h, sV.n)
                        sV.inc(v.tensor_copy(TMPE[:, c:c+1], PTS[:, 1:2]))

                @blk.vector
                def _(v):
                    v.wait_ge(sV.h, sV.n)
                    v = AutoV(v)
                    v.tensor_scalar(TMPA[:, 0:16], TMPD[:, 0:16], 1.0 / D, None,
                                    ALU.mult)
                    v.tensor_single_scalar(TMPA[:, 0:16], TMPA[:, 0:16],
                                           CONSC[:, 4:5], ALU.add)
                    rsqrt(v, TMPB[:, 0:16], TMPA[:, 0:16], TMPC[:, 0:16],
                          STATA[:, 0:16])
                    v.tensor_tensor(TMPC[:, 0:16], TMPB[:, 0:16], TMPB[:, 0:16],
                                    ALU.mult)
                    v.tensor_scalar(TMPD[:, 0:16], TMPE[:, 0:16], 1.0 / D, None,
                                    ALU.mult)
                    v.tensor_tensor(TMPC[:, 0:16], TMPC[:, 0:16], TMPD[:, 0:16],
                                    ALU.mult)
                    v.tensor_single_scalar(TMPC[:, 0:16], TMPC[:, 0:16],
                                           CONSC[:, 3:4], ALU.add)
                    rsqrt(v, TMPD[:, 0:16], TMPC[:, 0:16], TMPA[:, 0:16],
                          STATA[:, 0:16])
                    v.tensor_tensor(RS[:, 0:16], TMPB[:, 0:16], TMPD[:, 0:16],
                                    ALU.mult)
                    v.tensor_single_scalar(RS[:, 0:16], RS[:, 0:16], MPH[:, 0:1],
                                           ALU.mult)
                    for h in range(H):
                        v.tensor_single_scalar(OALL[h][:, :], OALL[h][:, :],
                                               W12S[:, h:h+1], ALU.mult)
                rs_v = sV.n

            with ExitStack() as stJ:
                EJ = stJ.enter_context
                PJW = EJ(nc.sbuf_tensor("PJW", [128, NCH * 512], F32))
                PS1 = EJ(nc.sbuf_tensor("PS1", [128, NCH * 512], BF16))
                PJQ = EJ(nc.sbuf_tensor("PJQ", [128, NCH * 512], BF16))
                X2SL = [EJ(nc.sbuf_tensor(f"X2SL{i}", [128, 512], F32)) for i in range(2)]
                X3T = [EJ(nc.sbuf_tensor(f"X3T{i}", [128, 512], F32)) for i in range(2)]
                PF3 = [EJ(nc.psum_tensor(f"PF3{i}", [128, 512], F32)) for i in range(2)]

                for sl in range(4):
                    @blk.gpsimd
                    def _(g, sl=sl, rs_v=rs_v):
                        if sl == 0:
                            g.wait_ge(sV.h, rs_v)
                        else:
                            g.wait_ge(sS.h, sS.n)
                        sPW.inc(g.dma_start(out=PJW[:, :],
                                            in_=proj_t[sl].transpose([1, 0, 2])), 16)
                    pw_ld = sPW.n

                    @blk.scalar
                    def _(s, sl=sl, pw_ld=pw_ld):
                        s.wait_ge(sPW.h, pw_ld)
                        if sl >= 1:
                            s.wait_ge(sV.h, sV.n)
                        s.activation(PS1[:, :], PJW[:, :], AF.Sign,
                                     bias=NEGH[:, 0:1], scale=SCLP[:, 0:1])
                        sS.inc(s.activation(PJQ[:, :], PJW[:, :], AF.Sign,
                                            bias=POSH[:, 0:1], scale=SCLP[:, 0:1]))
                    psgn = sS.n

                    @blk.vector
                    def _(v, sl=sl, psgn=psgn):
                        v.wait_ge(sS.h, psgn)
                        if sl >= 1:
                            v.wait_ge(sT.h, sT.n)
                        sV.inc(v.tensor_tensor(PJQ[:, :], PJQ[:, :], PS1[:, :],
                                               ALU.add))
                    pjq_v = sV.n

                    for c in range(NT):
                        @blk.sync
                        def _(sy, sl=sl, c=c, rs_v=rs_v):
                            if c == 0 and sl == 0:
                                sy.wait_ge(sV.h, rs_v)
                            if c >= 2 or sl > 0:
                                sy.wait_ge(sV.h, sV.n)
                            sX2[c % 2].inc(sy.dma_start(
                                out=X2SL[c % 2][:, :],
                                in_=x2_h[c][:, sl*512:(sl+1)*512]), 16)
                        xld = sX2[c % 2].n

                        @blk.tensor
                        def _(t, sl=sl, c=c, pjq_v=pjq_v):
                            t.wait_ge(sV.h, max(pjq_v,
                                                sV.n if (c >= 2 or sl > 0) else 0))
                            for hh in range(H):
                                ins = t.matmul(PF3[c % 2][:, :],
                                               OALL[hh][:, c*128:(c+1)*128],
                                               PJQ[:, hh*512:(hh+1)*512],
                                               start=(hh == 0), stop=(hh == H - 1))
                            sT.inc(ins)
                        mmd = sT.n

                        @blk.vector
                        def _(v, sl=sl, c=c, mmd=mmd, xld=xld):
                            v.wait_ge(sT.h, mmd)
                            v.wait_ge(sX2[c % 2].h, xld)
                            if c >= 2 or sl > 0:
                                v.wait_ge(sX3.h, sX3.n)
                            v.wait_ge(sV.h, sV.n)
                            sV.inc(v.scalar_tensor_tensor(
                                X3T[c % 2][:, :], PF3[c % 2][:, :], RS[:, c:c+1],
                                X2SL[c % 2][:, :], ALU.mult, ALU.add))
                        ev3 = sV.n

                        @blk.sync
                        def _(sy, sl=sl, c=c, ev3=ev3):
                            sy.wait_ge(sV.h, ev3)
                            sX3.inc(sy.dma_start(out=x3_dst[c][:, sl*512:(sl+1)*512],
                                                 in_=X3T[c % 2][:, :]), 16)
        x3_stores = sX3.n
        stCC.close()

        if stage == 3:
            @blk.sync
            def _(sy):
                sy.wait_ge(sX3.h, x3_stores)
            return nc

        stX2 = ExitStack()
        XRES2 = stX2.enter_context(nc.sbuf_tensor("XRES2", [128, NT * 2048], F32))
        sXR = Sem(st, nc, "sXR")

        @blk.sync
        def _(sy):
            sy.wait_ge(sX3.h, x3_stores)
            for i in range(NT):
                sXR.inc(sy.dma_start(out=XRES2[:, i*2048:(i+1)*2048], in_=x3_h[i]), 16)
        x3l = sXR.n

        @blk.vector
        def _(v):
            v.wait_ge(sXR.h, x3l)
            sV.inc(v.engine_nop())
        x3_v = sV.n

        x4_v = emit_ffn(x3_v, XRES2)

        @blk.sync
        def _(sy):
            sy.wait_ge(sV.h, x4_v)
            for i in range(NT):
                sOUT.inc(sy.dma_start(out=out_t[i], in_=XRES2[:, i*2048:(i+1)*2048]),
                         16)
            sy.wait_ge(sOUT.h, sOUT.n)
        stX2.close()
        return nc

    return nc


def _host_layout(inputs, core):
    b, s = core // 2, core % 2
    x = np.asarray(inputs["L0_attn_a_x"])
    Wq = np.asarray(inputs["L0_attn_a_Wq"]); Wk = np.asarray(inputs["L0_attn_a_Wk"])
    Wv = np.asarray(inputs["L0_attn_a_Wv"]); Wo = np.asarray(inputs["L0_attn_a_Wo"])
    bo = np.asarray(inputs["L0_attn_a_bo"])
    qkv_w = np.asarray(inputs["L0_attn_b_qkv_w"])
    proj_w = np.asarray(inputs["L0_attn_b_proj_w"])
    f8 = np.asarray(inputs["L0_ffn_weight"])
    if f8.dtype != ml_dtypes.float8_e4m3fn:
        f8 = f8.view(ml_dtypes.float8_e4m3fn)

    d = {}
    d["xo"] = np.ascontiguousarray(x[b, s*2048:(s+1)*2048])
    d["xr"] = np.ascontiguousarray(x[b, (1-s)*2048:(2-s)*2048])
    wqT = Wq.T
    d["wq_t"] = np.ascontiguousarray(wqT.reshape(16, 128, 16, 128).transpose(2, 0, 1, 3))
    d["wkv_t"] = np.ascontiguousarray(
        np.concatenate([Wk.T, Wv.T], axis=1).reshape(16, 128, 256))
    d["wo_t"] = np.ascontiguousarray(Wo.T.reshape(16, 128, 4, 512).transpose(2, 0, 1, 3))
    d["bo"] = np.ascontiguousarray(np.broadcast_to(bo[None, :], (128, 2048)))
    qk = qkv_w.T.reshape(16, 128, 3, 16, 128)
    d["qkv_t"] = np.ascontiguousarray(
        qk.transpose(3, 0, 1, 2, 4).reshape(16, 16, 128, 384))
    d["proj_t"] = np.ascontiguousarray(
        proj_w.T.reshape(16, 128, 4, 512).transpose(2, 0, 1, 3))
    f8T = np.ascontiguousarray(f8.T)
    d["f8_t"] = np.ascontiguousarray(
        f8T.reshape(16, 128, 4, 512).transpose(2, 0, 1, 3)).view(ml_dtypes.float8_e4m3)
    d["wscale"] = np.ascontiguousarray(np.broadcast_to(
        np.asarray(inputs["L0_ffn_weight_scale"]).reshape(1, 1), (128, 1)))
    isc = np.asarray(inputs["L0_ffn_input_scale"])
    if isc.dtype != ml_dtypes.bfloat16:
        isc = isc.view(ml_dtypes.bfloat16)
    d["iscale"] = np.ascontiguousarray(np.broadcast_to(isc.reshape(1, 1), (128, 1)))
    d["s_cols"] = np.ascontiguousarray(
        np.broadcast_to(np.asarray(inputs["L0_attn_b_s"])[None, :], (128, 16)))
    d["qnw"] = np.ascontiguousarray(
        np.asarray(inputs["L0_attn_b_qkv_norm_w"]).reshape(16, 128).T)
    d["nw"] = np.ascontiguousarray(
        np.asarray(inputs["L0_attn_b_norm_w"]).reshape(16, 128).T)
    d["pnw"] = np.ascontiguousarray(
        np.asarray(inputs["L0_attn_b_proj_norm_w"]).reshape(16, 128).T)
    d["smask"] = np.full((128, 1), float(s), dtype=np.float32)
    return d


def kernel(**inputs):
    nc = build(STAGE)
    in_maps = [_host_layout(inputs, c) for c in range(8)]
    res = run_bass_kernel_spmd(nc, in_maps, core_ids=list(range(8)))
    out = np.empty((4, 4096, 2048), np.float32)
    for c in range(8):
        b, s = c // 2, c % 2
        out[b, s*2048:(s+1)*2048] = res.results[c]["out"]
    return out


# version 12 (final)
# speedup vs baseline: 1.0073x; 1.0073x over previous
import numpy as np
import ml_dtypes
from contextlib import ExitStack

import concourse.bass as bass
import concourse.mybir as mybir
from concourse.bass_utils import run_bass_kernel_spmd
from concourse import dve_ops as DVO

F32 = mybir.dt.float32
BF16 = mybir.dt.bfloat16
FP8 = mybir.dt.float8e4
U8 = mybir.dt.uint8
I32 = mybir.dt.int32
AF = mybir.ActivationFunctionType
ALU = mybir.AluOpType
AX = mybir.AxisListType

H, DH, D = 16, 128, 2048
TC, NT, NCH, NJ = 2048, 16, 16, 16
EPS_LN = 1e-5
EPS32 = float(np.finfo(np.float32).eps)
ASCALE = float(np.float32(1.0) / np.sqrt(np.float32(128.0)))
OSCALE = float(np.float32(128.0 ** -0.5))

STAGE = 4


class Sem:
    def __init__(self, st, nc, name):
        self.h = st.enter_context(nc.semaphore(name))
        self.n = 0

    def inc(self, ins, k=1):
        ins.then_inc(self.h, k)
        self.n += k
        return ins


def build(stage=STAGE):
    nc = bass.Bass("TRN2", target_bir_lowering=False)

    def din(name, shape, dt=F32):
        return nc.dram_tensor(name, shape, dt, kind="ExternalInput").ap()

    xo = din("xo", [TC, D])
    xr = din("xr", [TC, D])
    wq_t = din("wq_t", [H, NCH, 128, 128])
    wkv_t = din("wkv_t", [NCH, 128, 256])
    wo_t = din("wo_t", [4, NCH, 128, 512])
    bo = din("bo", [128, D])
    qkv_t = din("qkv_t", [H, NCH, 128, 384])
    proj_t = din("proj_t", [4, NCH, 128, 512])
    f8_t = din("f8_t", [4, NCH, 128, 512], FP8)
    wscale = din("wscale", [128, 1])
    iscale = din("iscale", [128, 1], BF16)
    s_cols = din("s_cols", [128, H])
    qnw = din("qnw", [128, NCH])
    nw = din("nw", [128, NCH])
    pnw = din("pnw", [128, NCH])
    smask = din("smask", [128, 1])

    out_hbm = nc.dram_tensor("out", [TC, D], F32, kind="ExternalOutput").ap()

    ctx_h = nc.dram_tensor("ctx_h", [NT, H, 128, 128], BF16).ap()
    qkv_h = nc.dram_tensor("qkv_h", [H, 4, 128, TC], BF16).ap()
    x2_h = nc.dram_tensor("x2_h", [NT, 128, D], F32).ap()
    x3_h = nc.dram_tensor("x3_h", [NT, 128, D], F32).ap()
    fw_bf = nc.dram_tensor("fw_bf", [4, 128, NCH * 512], BF16).ap()
    cc_in = nc.dram_tensor("cc_in", [128, H * 128], F32)
    cc_out = nc.dram_tensor("cc_out", [2, 128, H * 128], F32)

    xo_t = xo.rearrange("(t p) d -> t p d", p=128)
    xr_t = xr.rearrange("(t p) d -> t p d", p=128)
    out_t = out_hbm.rearrange("(t p) d -> t p d", p=128)

    st = ExitStack()
    with st:
        E = st.enter_context
        sIN = Sem(st, nc, "sIN")
        sSP = Sem(st, nc, "sSP")
        sV = Sem(st, nc, "sV")
        sS = Sem(st, nc, "sS")
        sT = Sem(st, nc, "sT")
        sG = Sem(st, nc, "sG")
        sCC = Sem(st, nc, "sCC")
        sGD = Sem(st, nc, "sGD")
        sXL = [Sem(st, nc, f"sXL{i}") for i in range(2)]
        sCX = [Sem(st, nc, f"sCX{i}") for i in range(2)]
        sWQ = [Sem(st, nc, f"sWQ{i}") for i in range(2)]
        sWO = Sem(st, nc, "sWO")
        sXP = Sem(st, nc, "sXP")
        sOUT = Sem(st, nc, "sOUT")
        sAB = [Sem(st, nc, f"sAB{i}") for i in range(2)]

        blk = E(nc.Block())

        SCRAP = E(nc.sbuf_tensor("SCRAP", [128, 2048], F32))
        HBF = E(nc.sbuf_tensor("HBF", [128, 2048], BF16))
        STATA = E(nc.sbuf_tensor("STATA", [128, 16], F32))
        STATB = E(nc.sbuf_tensor("STATB", [128, 16], F32))
        NSC = E(nc.sbuf_tensor("NSC", [128, 16], F32))
        NBI = E(nc.sbuf_tensor("NBI", [128, 16], F32))
        TMPA = E(nc.sbuf_tensor("TMPA", [128, 16], F32))
        TMPB = E(nc.sbuf_tensor("TMPB", [128, 16], F32))
        TMPC = E(nc.sbuf_tensor("TMPC", [128, 16], F32))
        TMPD = E(nc.sbuf_tensor("TMPD", [128, 16], F32))
        TMPE = E(nc.sbuf_tensor("TMPE", [128, 16], F32))
        ONESB = E(nc.sbuf_tensor("ONESB", [128, 1], BF16))
        ONES1F = E(nc.sbuf_tensor("ONES1F", [1, 128], F32))
        ONES1B = E(nc.sbuf_tensor("ONES1B", [1, 128], BF16))
        IDENT = E(nc.sbuf_tensor("IDENT", [128, 128], BF16))
        IDENTF = E(nc.sbuf_tensor("IDENTF", [128, 128], F32))
        CONSC = E(nc.sbuf_tensor("CONSC", [128, 8], F32))
        SCL = E(nc.sbuf_tensor("SCL", [128, 8], F32))
        ISC = E(nc.sbuf_tensor("ISC", [128, 1], BF16))
        SCOL = E(nc.sbuf_tensor("SCOL", [128, H], F32))
        QNW = E(nc.sbuf_tensor("QNW", [128, NCH], F32))
        NW = E(nc.sbuf_tensor("NW", [128, NCH], F32))
        PNW = E(nc.sbuf_tensor("PNW", [128, NCH], F32))
        W12S = E(nc.sbuf_tensor("W12S", [128, NCH], F32))
        W1SQ = E(nc.sbuf_tensor("W1SQ", [128, 2 * H], BF16))
        SMK = E(nc.sbuf_tensor("SMK", [128, 1], F32))
        MASKT = E(nc.sbuf_tensor("MASKT", [128, H * 128], BF16))
        LAMQ = E(nc.sbuf_tensor("LAMQ", [128, H * 128], BF16))
        LAMRV = E(nc.sbuf_tensor("LAMRV", [128, H], F32))
        LAMTT = E(nc.sbuf_tensor("LAMTT", [128, H], F32))
        LAMJ = E(nc.sbuf_tensor("LAMJ", [128, H * NJ], F32))
        ABSD = E(nc.sbuf_tensor("ABSD", [128, 16], F32))
        ABSE = E(nc.sbuf_tensor("ABSE", [128, 8], F32))
        DIFFT = E(nc.sbuf_tensor("DIFFT", [128, 128], F32))
        INDC = E(nc.sbuf_tensor("INDC", [128, 128], F32))


        class AutoV:
            def __init__(self, v):
                self._v = v

            def __getattr__(self, k):
                f = getattr(self._v, k)
                def g(*a, **kw):
                    r = f(*a, **kw)
                    sV.inc(self._v.engine_nop())
                    self._v.wait_ge(sV.h, sV.n)
                    return r
                return g

        def rsqrt(v, dst, src, t1, t2):
            di = dst.bitcast(I32)
            si = src.bitcast(I32)
            v.tensor_scalar(di, si, 1, None, ALU.logical_shift_right)
            v.tensor_scalar(di, di, -1, 0x5F3759DF, ALU.mult, ALU.add)
            for _ in range(3):
                v.tensor_tensor(t1, dst, dst, ALU.mult)
                v.tensor_tensor(t1, t1, src, ALU.mult)
                v.tensor_scalar(t1, t1, -0.5, 1.5, ALU.mult, ALU.add)
                v.tensor_tensor(dst, dst, t1, ALU.mult)

        @blk.sync
        def _(sy):
            sIN.inc(sy.dma_start(out=SCOL[:, :], in_=s_cols), 16)
            sIN.inc(sy.dma_start(out=QNW[:, :], in_=qnw), 16)
            sIN.inc(sy.dma_start(out=NW[:, :], in_=nw), 16)
            sIN.inc(sy.dma_start(out=PNW[:, :], in_=pnw), 16)
            sIN.inc(sy.dma_start(out=SMK[:, :], in_=smask), 16)
            sIN.inc(sy.dma_start(out=ISC[:, :], in_=iscale), 16)
            sIN.inc(sy.dma_start(out=SCL[:, 2:3], in_=wscale), 16)
        p0_in = sIN.n

        @blk.gpsimd
        def _(g):
            g.memset(ONESB[:, :], 1.0)
            g.memset(ONES1F[:, :], 1.0)
            g.memset(ONES1B[:, :], 1.0)
            g.memset(CONSC[:, 0:1], -0.5)
            g.memset(CONSC[:, 1:2], 0.5)
            g.memset(CONSC[:, 2:3], EPS_LN)
            g.memset(CONSC[:, 3:4], EPS32)
            g.memset(CONSC[:, 4:5], 1e-6)
            sG.inc(g.memset(IDENT[:, :], 0.0))
            sG.inc(g.memset(IDENTF[:, :], 0.0))
            sG.inc(g.memset(INDC[:, :], 1.0))
            g.wait_ge(sG.h, sG.n)
            g.affine_select(out=IDENTF[:, :], in_=IDENTF[:, :],
                            compare_op=ALU.not_equal, fill=1.0, base=0,
                            pattern=[[-1, 128]], channel_multiplier=1)
            g.affine_select(out=IDENT[:, :], in_=IDENT[:, :],
                            compare_op=ALU.not_equal, fill=1.0, base=0,
                            pattern=[[-1, 128]], channel_multiplier=1)
            g.iota(DIFFT[:, :], pattern=[[1, 128]], base=0, channel_multiplier=-1,
                   allow_small_or_imprecise_dtypes=True)
            g.affine_select(out=INDC[:, :], in_=INDC[:, :],
                            compare_op=ALU.is_ge, fill=0.0, base=0,
                            pattern=[[1, 128]], channel_multiplier=-1)
            sG.inc(g.iota(SCRAP[:, 0:H], pattern=[[0, H]], base=0,
                          channel_multiplier=1, allow_small_or_imprecise_dtypes=True))
        p0_g = sG.n

        @blk.vector
        def _(v):
            v.wait_ge(sIN.h, p0_in)
            v.wait_ge(sG.h, p0_g)
            v = AutoV(v)
            v.tensor_copy(SCL[:, 0:1], ISC[:, :])
            v.tensor_tensor(SCL[:, 1:2], SCL[:, 0:1], SCL[:, 2:3], ALU.mult)
            v.reciprocal(SCL[:, 0:1], SCL[:, 0:1])
            v.tensor_tensor(W12S[:, :], NW[:, :], PNW[:, :], ALU.mult)
            v.tensor_tensor(TMPA[:, 0:16], NW[:, :], NW[:, :], ALU.mult)
            for h in range(H):
                v.tensor_copy(W1SQ[:, 2*h:2*h+1], ONESB[:, 0:1])
                v.tensor_copy(W1SQ[:, 2*h+1:2*h+2], TMPA[:, h:h+1])
            v.tensor_scalar(TMPB[:, 0:H], SCRAP[:, 0:H], -128.0, None, ALU.add)
            v.tensor_tensor(TMPB[:, 0:H], TMPB[:, 0:H], SCOL[:, :], ALU.mult)
            v.tensor_scalar(TMPC[:, 0:H], SCOL[:, :], -128.0, None, ALU.mult)
            sV.inc(v.engine_nop())
        p0_v = sV.n

        @blk.scalar
        def _(s):
            s.preload_act_table(AF.Exp)
            s.wait_ge(sV.h, p0_v)
            s.activation(LAMRV[:, :], TMPB[:, 0:H], AF.Exp)
            sS.inc(s.activation(LAMTT[:, :], TMPC[:, 0:H], AF.Exp))
        p0_s = sS.n

        @blk.vector
        def _(v):
            v.wait_ge(sS.h, p0_s)
            v = AutoV(v)
            dif_b = DIFFT[:, :].unsqueeze(1).broadcast_to([128, H, 128])
            sc_b = SCOL[:, :].unsqueeze(2).broadcast_to([128, H, 128])
            m3 = SCRAP[:, :].rearrange("p (h t) -> p h t", t=128)
            v.scalar_tensor_tensor(m3, dif_b, -1.0, sc_b, ALU.mult, ALU.mult)
        p0_v2 = sV.n

        @blk.scalar
        def _(s):
            s.wait_ge(sV.h, p0_v2)
            sS.inc(s.activation(SCRAP[:, :], SCRAP[:, :], AF.Exp))
        p0_s2 = sS.n

        @blk.vector
        def _(v):
            v.wait_ge(sS.h, p0_s2)
            for h in range(H):
                v.tensor_tensor(MASKT[:, h*128:(h+1)*128], SCRAP[:, h*128:(h+1)*128],
                                INDC[:, :], ALU.mult)
            sV.inc(v.engine_nop())
            v.wait_ge(sV.h, sV.n)
        p0_v3 = sV.n

        @blk.gpsimd
        def _(g):
            g.wait_ge(sV.h, p0_v3)
            sG.inc(g.iota(SCRAP[:, :], pattern=[[0, H], [1, 128]], base=0,
                          channel_multiplier=0, allow_small_or_imprecise_dtypes=True))
        p0_g2 = sG.n

        @blk.vector
        def _(v):
            v.wait_ge(sG.h, p0_g2)
            v = AutoV(v)
            t3 = SCRAP[:, :].rearrange("p (h t) -> p h t", t=128)
            sc_b = SCOL[:, :].unsqueeze(2).broadcast_to([128, H, 128])
            v.scalar_tensor_tensor(t3, t3, -1.0, sc_b, ALU.mult, ALU.mult)
        p0_v4 = sV.n

        @blk.scalar
        def _(s):
            s.wait_ge(sV.h, p0_v4)
            sS.inc(s.activation(LAMQ[:, :], SCRAP[:, :], AF.Exp))
        p0_done = sS.n

        @blk.gpsimd
        def _(g):
            sG.inc(g.iota(LAMJ[:, :], pattern=[[0, H], [1, NJ]], base=0,
                          channel_multiplier=0,
                          allow_small_or_imprecise_dtypes=True))
        lamj_g = sG.n

        @blk.vector
        def _(v, lamj_g=lamj_g):
            v.wait_ge(sG.h, lamj_g)
            v.wait_ge(sS.h, p0_done)
            v = AutoV(v)
            v.tensor_scalar(LAMJ[:, :], LAMJ[:, :], 128.0, None, ALU.mult)
            v.tensor_scalar(LAMJ[:, :], LAMJ[:, :], -2048.0 + 128.0, None, ALU.add)
            l3 = LAMJ[:, :].rearrange("p (h j) -> p h j", j=NJ)
            sc_b = SCOL[:, :].unsqueeze(2).broadcast_to([128, H, NJ])
            v.scalar_tensor_tensor(l3, l3, 1.0, sc_b, ALU.mult, ALU.mult)
        lamj_v = sV.n

        @blk.scalar
        def _(s, lamj_v=lamj_v):
            s.wait_ge(sV.h, lamj_v)
            sS.inc(s.activation(LAMJ[:, :], LAMJ[:, :], AF.Exp))
        p0_done = sS.n

        def ln_stats_tile(v, x_ap, i):
            sV.inc(v.scalar_tensor_tensor(SCRAP[:, :], x_ap, 1.0, x_ap, ALU.mult,
                                          ALU.mult, accum_out=STATA[:, i:i+1]))
            v.wait_ge(sV.h, sV.n)
            sV.inc(v.tensor_scalar(SCRAP[:, :], x_ap, 0.0, 0.0, ALU.add, ALU.add,
                                   accum_out=STATB[:, i:i+1]))
            v.wait_ge(sV.h, sV.n)

        def ln_finalize(v, n=16, rms2=False, fold_recip_s=False):
            v = AutoV(v)
            v.tensor_scalar(TMPA[:, 0:n], STATB[:, 0:n], 1.0 / D, None, ALU.mult)
            v.tensor_scalar(TMPB[:, 0:n], STATA[:, 0:n], 1.0 / D, None, ALU.mult)
            v.tensor_tensor(TMPC[:, 0:n], TMPA[:, 0:n], TMPA[:, 0:n], ALU.mult)
            v.tensor_tensor(TMPB[:, 0:n], TMPB[:, 0:n], TMPC[:, 0:n], ALU.subtract)
            v.tensor_single_scalar(TMPC[:, 0:n], TMPB[:, 0:n], CONSC[:, 2:3], ALU.add)
            rsqrt(v, NSC[:, 0:n], TMPC[:, 0:n], TMPD[:, 0:n], TMPE[:, 0:n])
            if rms2:
                v.reciprocal(TMPD[:, 0:n], TMPC[:, 0:n])
                v.tensor_tensor(TMPD[:, 0:n], TMPD[:, 0:n], TMPB[:, 0:n], ALU.mult)
                v.tensor_single_scalar(TMPD[:, 0:n], TMPD[:, 0:n], CONSC[:, 3:4], ALU.add)
                rsqrt(v, TMPB[:, 0:n], TMPD[:, 0:n], TMPC[:, 0:n], TMPE[:, 0:n])
                v.tensor_tensor(NSC[:, 0:n], NSC[:, 0:n], TMPB[:, 0:n], ALU.mult)
            if fold_recip_s:
                v.tensor_single_scalar(NSC[:, 0:n], NSC[:, 0:n], SCL[:, 0:1], ALU.mult)
            v.scalar_tensor_tensor(NBI[:, 0:n], TMPA[:, 0:n], -1.0, NSC[:, 0:n],
                                   ALU.mult, ALU.mult)

        with ExitStack() as stA:
            EA = stA.enter_context
            KT = EA(nc.sbuf_tensor("KT", [128, 2 * TC], BF16))
            VT = EA(nc.sbuf_tensor("VT", [128, 2 * TC], BF16))
            QT = [EA(nc.sbuf_tensor(f"QT{h}", [128, TC], BF16)) for h in range(H)]
            with ExitStack() as stA1:
                EA1 = stA1.enter_context
                HTO = [EA1(nc.sbuf_tensor(f"HTO{c}", [128, TC], BF16)) for c in range(NCH)]
                XTI = [EA1(nc.sbuf_tensor(f"XTI{i}", [128, D], F32)) for i in range(2)]
                WQH = [EA1(nc.sbuf_tensor(f"WQH{i}", [128, D], BF16)) for i in range(2)]
                WKV = EA1(nc.sbuf_tensor("WKV", [128, NCH * 256], BF16))
                PTR = [EA1(nc.psum_tensor(f"PTR{i}", [128, 128], BF16)) for i in range(4)]
                PRJ = [EA1(nc.psum_tensor(f"PRJ{i}", [128, 512], F32)) for i in range(2)]
                PVV = [EA1(nc.psum_tensor(f"PVV{i}", [128, 128], F32)) for i in range(2)]

                @blk.gpsimd
                def _(g):
                    g.wait_ge(sS.h, p0_done)
                    sGD.inc(g.dma_start(
                        out=WKV[:, :], in_=wkv_t.transpose([1, 0, 2])), 16)
                    sWQ[0].inc(g.dma_start(
                        out=WQH[0][:, :],
                        in_=wq_t[0].transpose([1, 0, 2])), 16)
                g_wkv = sGD.n
                g_wq0 = sWQ[0].n

                for half in range(2):
                    x_src = xo_t if half == 0 else xr_t
                    for i in range(NT):
                        @blk.sync
                        def _(sy, i=i, x_src=x_src):
                            if i >= 1:
                                sy.wait_ge(sV.h, sV.n)
                            sXL[i % 2].inc(sy.dma_start(out=XTI[i % 2][:, :],
                                                        in_=x_src[i]), 16)
                        ld = sXL[i % 2].n

                        @blk.vector
                        def _(v, i=i, ld=ld, half=half):
                            if i == 0 and half == 0:
                                v.wait_ge(sS.h, p0_done)
                            v.wait_ge(sXL[i % 2].h, ld)
                            ln_stats_tile(v, XTI[i % 2][:, :], i)
                            sV.inc(v.engine_nop())

                    @blk.vector
                    def _(v):
                        ln_finalize(v, 16)
                        sV.inc(v.engine_nop())
                    fin = sV.n

                    for i in range(NT):
                        @blk.sync
                        def _(sy, i=i, x_src=x_src, fin=fin):
                            sy.wait_ge(sV.h, max(fin, sV.n if i >= 1 else 0))
                            sXL[i % 2].inc(sy.dma_start(out=XTI[i % 2][:, :],
                                                        in_=x_src[i]), 16)
                        ld = sXL[i % 2].n

                        @blk.vector
                        def _(v, i=i, ld=ld, fin=fin):
                            v.wait_ge(sXL[i % 2].h, ld)
                            if i >= 1:
                                v.wait_ge(sT.h, sT.n)
                            sV.inc(v.tensor_scalar(HBF[:, :], XTI[i % 2][:, :],
                                                   NSC[:, i:i+1], NBI[:, i:i+1],
                                                   ALU.mult, ALU.add))
                        nrm = sV.n

                        base_v = sV.n

                        @blk.tensor
                        def _(t, i=i, nrm=nrm, base_v=base_v):
                            t.wait_ge(sV.h, nrm)
                            for c in range(NCH):
                                if c >= 4:
                                    t.wait_ge(sV.h, base_v + c - 3)
                                sT.inc(t.transpose(PTR[c % 4][:, :],
                                                   HBF[:, c*128:(c+1)*128], IDENT[:, :]))
                        tr = sT.n

                        @blk.vector
                        def _(v, i=i, half=half, tr=tr):
                            for c in range(NCH):
                                v.wait_ge(sT.h, tr - 15 + c)
                                sV.inc(v.tensor_copy(HTO[c][:, i*128:(i+1)*128],
                                                     PTR[c % 4][:, :]))
                    htod = sV.n

                    kt_base = sV.n
                    for qs in range(4):
                        @blk.tensor
                        def _(t, qs=qs, htod=htod, kt_base=kt_base):
                            t.wait_ge(sV.h, max(htod, kt_base) if qs < 2 else sV.n)
                            t.wait_ge(sGD.h, g_wkv)
                            for c in range(NCH):
                                ins = t.matmul(PRJ[qs % 2][:, :], WKV[:, c*256:c*256+128],
                                               HTO[c][:, qs*512:(qs+1)*512],
                                               start=(c == 0), stop=(c == NCH - 1))
                            sT.inc(ins)
                        mmd = sT.n

                        @blk.vector
                        def _(v, qs=qs, half=half, mmd=mmd):
                            v.wait_ge(sT.h, mmd)
                            sV.inc(v.tensor_copy(
                                KT[:, half*2048 + qs*512: half*2048 + (qs+1)*512],
                                PRJ[qs % 2][:, :]))

                    vv_base = sV.n
                    for i in range(NT):
                        @blk.tensor
                        def _(t, i=i, vv_base=vv_base):
                            t.wait_ge(sV.h, sV.n if i >= 2 else vv_base)
                            for c in range(NCH):
                                ins = t.matmul(PVV[i % 2][:, :], HTO[c][:, i*128:(i+1)*128],
                                               WKV[:, c*256+128:c*256+256],
                                               start=(c == 0), stop=(c == NCH - 1))
                            sT.inc(ins)
                        mmd = sT.n

                        @blk.vector
                        def _(v, i=i, half=half, mmd=mmd):
                            v.wait_ge(sT.h, mmd)
                            sV.inc(v.tensor_copy(
                                VT[:, (half*16 + i)*128:(half*16 + i + 1)*128],
                                PVV[i % 2][:, :]))

                    if half == 0:
                        for h in range(H):
                            @blk.gpsimd
                            def _(g, h=h):
                                if h + 1 < H:
                                    if h >= 1:
                                        g.wait_ge(sT.h, sT.n)
                                    sWQ[(h+1) % 2].inc(g.dma_start(
                                        out=WQH[(h+1) % 2][:, :],
                                        in_=wq_t[h+1].transpose([1, 0, 2])), 16)
                            wq_ld = sWQ[h % 2].n

                            q_base = sV.n
                            for qs in range(4):
                                @blk.tensor
                                def _(t, h=h, qs=qs, wq_ld=wq_ld, q_base=q_base):
                                    if qs == 0:
                                        t.wait_ge(sWQ[h % 2].h, wq_ld)
                                    t.wait_ge(sV.h, sV.n if qs >= 2 else q_base)
                                    for c in range(NCH):
                                        ins = t.matmul(PRJ[qs % 2][:, :],
                                                       WQH[h % 2][:, c*128:(c+1)*128],
                                                       HTO[c][:, qs*512:(qs+1)*512],
                                                       start=(c == 0),
                                                       stop=(c == NCH - 1))
                                    sT.inc(ins)
                                mmd = sT.n

                                @blk.vector
                                def _(v, h=h, qs=qs, mmd=mmd):
                                    v.wait_ge(sT.h, mmd)
                                    sV.inc(v.tensor_copy(QT[h][:, qs*512:(qs+1)*512],
                                                         PRJ[qs % 2][:, :]))
            projs_v = sV.n

            sFC = Sem(st, nc, "sFCg")
            for sl in range(4):
                for piece in range(4):
                    c0 = piece * (NCH // 4)
                    dst = fw_bf[sl][:, c0*512:(c0 + NCH // 4)*512] \
                        .rearrange("p (c w) -> c p w", w=512)
                    srcp = f8_t[sl, c0:c0 + NCH // 4]

                    @blk.gpsimd
                    def _(g, dst=dst, srcp=srcp):
                        sFC.inc(g.dma_start(out=dst, in_=srcp), 16)
            fc_all = sFC.n

            with ExitStack() as stA2:
                EA2 = stA2.enter_context
                ET = [EA2(nc.sbuf_tensor(f"ET{i}", [128, 1024], BF16)) for i in range(4)]
                CTXH = EA2(nc.sbuf_tensor("CTXH", [128, TC], BF16))
                RECR = EA2(nc.sbuf_tensor("RECR", [1, 1024], F32))
                RSSB = EA2(nc.sbuf_tensor("RSSB", [1, 1024], F32))
                RSC2 = EA2(nc.sbuf_tensor("RSC2", [1, 1024], F32))
                CTXU = EA2(nc.sbuf_tensor("CTXU", [128, 1024], BF16))
                MST = EA2(nc.sbuf_tensor("MST", [1, 1], F32))
                ABSW = [EA2(nc.sbuf_tensor(f"ABSW{i}", [128, NCH * 384], F32))
                        for i in range(2)]

                with ExitStack() as stM:
                    PBM1 = stM.enter_context(nc.psum_tensor("PBM1", [128, 512], F32))
                    PBM2 = stM.enter_context(nc.psum_tensor("PBM2", [128, 1], F32))

                    @blk.tensor
                    def _(t):
                        t.wait_ge(sV.h, projs_v)
                        sT.inc(t.matmul(PBM1[:, :], KT[:, 0:128], QT[0][:, 0:512],
                                        start=True, stop=True))
                    m_mm = sT.n

                    @blk.vector
                    def _(v):
                        v.wait_ge(sT.h, m_mm)
                        sV.inc(v.reduce_max(MST[:, :], PBM1[0:1, 0:512], AX.X))
                    m_red = sV.n

                    @blk.tensor
                    def _(t):
                        t.wait_ge(sV.h, m_red)
                        sT.inc(t.matmul(PBM2[:, :], ONES1F[:, :], MST[:, :],
                                        start=True, stop=True))
                    m_bc = sT.n

                    @blk.vector
                    def _(v):
                        v.wait_ge(sT.h, m_bc)
                        sV.inc(v.tensor_scalar(SCL[:, 4:5], PBM2[:, :], -ASCALE,
                                               None, ALU.mult))
                    m_done = sV.n

                SCPS = EA2(nc.psum_tensor("SCPS", [128, 2048], F32))
                CTXPS = EA2(nc.psum_tensor("CTXPS", [128, 1024], F32))
                RSPS = EA2(nc.psum_tensor("RSPS", [1, 1024], F32))

                absr = [0] * (H + 8)
                prev_fin = m_done
                for h in range(H):
                    for qp in range(2):
                        us = (2 * qp, 2 * qp + 1)
                        base_t = sT.n
                        base_s = sS.n
                        prev_sp = sSP.n

                        @blk.tensor
                        def _(t, h=h, qp=qp, base_s=base_s, m_done=m_done,
                              prev_fin=prev_fin):
                            t.wait_ge(sV.h, m_done)
                            t.wait_ge(sS.h, base_s)
                            q0 = QT[h][:, (2*qp)*512:(2*qp+1)*512]
                            q1 = QT[h][:, (2*qp+1)*512:(2*qp+2)*512]
                            for kp in range(16):
                                if kp == 1:
                                    t.wait_ge(sV.h, prev_fin)
                                for s_id, q_ap in ((0, q0), (1, q1)):
                                    if kp >= 1:
                                        t.wait_ge(sS.h,
                                                  base_s + 2*(kp-1) + s_id + 1)
                                    pb = s_id * 1024
                                    t.matmul(SCPS[:, pb:pb+512],
                                             KT[:, (2*kp)*128:(2*kp+1)*128], q_ap,
                                             start=True, stop=True)
                                    sT.inc(t.matmul(
                                        SCPS[:, pb+512:pb+1024],
                                        KT[:, (2*kp+1)*128:(2*kp+2)*128], q_ap,
                                        start=True, stop=True))
                                if kp >= 1:
                                    for s_id in (0, 1):
                                        eb = ET[2*s_id + (kp-1) % 2]
                                        kc0 = 2*(kp-1)
                                        rsl = RSPS[0:1, s_id*512:(s_id+1)*512]
                                        csl = CTXPS[:, s_id*512:(s_id+1)*512]
                                        t.matmul(rsl, ONESB[:, :], eb[:, 0:512],
                                                 start=(kp == 1), stop=False)
                                        t.matmul(rsl, ONESB[:, :], eb[:, 512:1024],
                                                 start=False, stop=False)
                                        t.matmul(csl, VT[:, kc0*128:(kc0+1)*128],
                                                 eb[:, 0:512],
                                                 start=(kp == 1), stop=False)
                                        sT.inc(t.matmul(
                                            csl, VT[:, (kc0+1)*128:(kc0+2)*128],
                                            eb[:, 512:1024], start=False, stop=False))
                            for s_id in (0, 1):
                                t.wait_ge(sS.h, base_s + 31 + s_id)
                                eb = ET[2*s_id + 15 % 2]
                                rsl = RSPS[0:1, s_id*512:(s_id+1)*512]
                                csl = CTXPS[:, s_id*512:(s_id+1)*512]
                                t.matmul(rsl, ONESB[:, :], eb[:, 0:512],
                                         start=False, stop=False)
                                t.matmul(rsl, ONESB[:, :], eb[:, 512:1024],
                                         start=False, stop=True)
                                t.matmul(csl, VT[:, 30*128:31*128], eb[:, 0:512],
                                         start=False, stop=False)
                                sT.inc(t.matmul(csl, VT[:, 31*128:32*128],
                                                eb[:, 512:1024],
                                                start=False, stop=True))
                        inner_t = sT.n

                        @blk.scalar
                        def _(s, h=h, qp=qp, base_t=base_t, m_done=m_done):
                            s.wait_ge(sV.h, m_done)
                            for kp in range(16):
                                for s_id in (0, 1):
                                    tn = (s_id + 1) if kp == 0 else (4*kp - 1 + s_id)
                                    s.wait_ge(sT.h, base_t + tn)
                                    pb = s_id * 1024
                                    sS.inc(s.activation(
                                        ET[2*s_id + kp % 2][:, :],
                                        SCPS[:, pb:pb+1024], AF.Exp,
                                        bias=SCL[:, 4:5], scale=ASCALE))
                        inner_s = sS.n

                        @blk.vector
                        def _(v, h=h, qp=qp, inner_t=inner_t, prev_sp=prev_sp):
                            v.wait_ge(sT.h, inner_t)
                            if qp == 0:
                                v.wait_ge(sSP.h, prev_sp)
                            sV.inc(v.tensor_copy(RSSB[:, :], RSPS[0:1, :]))
                            sV.inc(v.tensor_copy(CTXU[:, :], CTXPS[:, :]))
                        ev_done = sV.n

                        @blk.vector
                        def _(v):
                            v.wait_ge(sV.h, ev_done)
                            sV.inc(v.reciprocal(RECR[:, 0:512], RSSB[:, 0:512]))
                        rec0 = sV.n

                        @blk.vector
                        def _(v):
                            sV.inc(v.reciprocal(RECR[:, 512:1024], RSSB[:, 512:1024]))
                        rec1 = sV.n

                        @blk.tensor
                        def _(t, rec0=rec0, rec1=rec1):
                            t.wait_ge(sV.h, rec0)
                            t.matmul(CTXPS[:, 0:512], ONES1F[:, :], RECR[:, 0:512],
                                     start=True, stop=True)
                            t.wait_ge(sV.h, rec1)
                            sT.inc(t.matmul(CTXPS[:, 512:1024], ONES1F[:, :],
                                            RECR[:, 512:1024], start=True, stop=True))
                        bc_done = sT.n

                        @blk.vector
                        def _(v, h=h, qp=qp, bc_done=bc_done):
                            v.wait_ge(sT.h, bc_done)
                            for s_id in (0, 1):
                                qs = 2*qp + s_id
                                sV.inc(v.tensor_tensor(
                                    CTXH[:, qs*512:(qs+1)*512],
                                    CTXU[:, s_id*512:(s_id+1)*512],
                                    CTXPS[:, s_id*512:(s_id+1)*512], ALU.mult))
                                v.wait_ge(sV.h, sV.n)
                        prev_fin = sV.n
                    ctx_done = sV.n

                    @blk.gpsimd
                    def _(g, h=h, ctx_done=ctx_done):
                        g.wait_ge(sV.h, ctx_done)
                        sSP.inc(g.dma_start(
                            out=ctx_h[:, h].transpose([1, 0, 2]),
                            in_=CTXH[:, :].rearrange("p (c n) -> p c n", n=128)), 16)

                    nb = 2 if h < 8 else 1
                    for _k in range(nb):
                        b = 2*h + _k if h < 8 else 8 + h
                        @blk.gpsimd
                        def _(g, b=b, m_done=m_done):
                            if b < 2:
                                g.wait_ge(sV.h, m_done)
                            else:
                                g.wait_ge(sV.h, absr[b - 2])
                            if b < H:
                                ap = qkv_t[b].transpose([1, 0, 2])
                                dst = ABSW[b % 2][:, :]
                            else:
                                sl, hf = (b - H) // 2, (b - H) % 2
                                ap = proj_t[sl].transpose([1, 0, 2])[
                                    :, :, hf*256:(hf+1)*256]
                                dst = ABSW[b % 2][:, 0:NCH*256]
                            sAB[b % 2].inc(g.dma_start(out=dst, in_=ap), 16)
                        ldb = sAB[b % 2].n

                        @blk.vector
                        def _(v, b=b, ldb=ldb):
                            v.wait_ge(sAB[b % 2].h, ldb)
                            if b < H:
                                sV.inc(v.tensor_reduce(ABSD[:, b:b+1],
                                                       ABSW[b % 2][:, :], AX.X,
                                                       ALU.add,
                                                       apply_absolute_value=True))
                            else:
                                sV.inc(v.tensor_reduce(ABSE[:, b-H:b-H+1],
                                                       ABSW[b % 2][:, 0:NCH*256],
                                                       AX.X, ALU.add,
                                                       apply_absolute_value=True))
                        absr[b] = sV.n
            attn_sp = sSP.n
            absr_all = sV.n

        stX = ExitStack()
        XRES = stX.enter_context(nc.sbuf_tensor("XRES", [128, NT * 2048], F32))

        with ExitStack() as stW:
            EW = stW.enter_context
            WOS = EW(nc.sbuf_tensor("WOS", [128, NCH * 512], BF16))
            BOF = EW(nc.sbuf_tensor("BOF", [128, 2048], F32))
            CTXC = [EW(nc.sbuf_tensor(f"CTXC{i}", [128, H * 128], BF16)) for i in range(2)]
            POW = [EW(nc.psum_tensor(f"POW{i}", [128, 512], F32)) for i in range(2)]

            @blk.sync
            def _(sy):
                sy.wait_ge(sV.h, absr_all)
                sy.wait_ge(sSP.h, attn_sp)
                sIN.inc(sy.dma_start(out=BOF[:, :], in_=bo), 16)
            bo_ld = sIN.n

            @blk.gpsimd
            def _(g):
                g.wait_ge(sSP.h, attn_sp)
                g.wait_ge(sV.h, absr_all)
                for i in range(NT):
                    sXP.inc(g.dma_start(out=XRES[:, i*2048:(i+1)*2048], in_=xo_t[i]),
                            16)
            x_pre = sXP.n

            @blk.vector
            def _(v, x_pre=x_pre):
                v.wait_ge(sIN.h, bo_ld)
                v.wait_ge(sXP.h, x_pre)
                for i in range(NT):
                    sV.inc(v.tensor_tensor(XRES[:, i*2048:(i+1)*2048],
                                           XRES[:, i*2048:(i+1)*2048],
                                           BOF[:, :], ALU.add))
                    v.wait_ge(sV.h, sV.n)
            bo_cast = sV.n

            for sl in range(4):
                @blk.gpsimd
                def _(g, sl=sl):
                    if sl > 0:
                        g.wait_ge(sT.h, sT.n)
                    else:
                        g.wait_ge(sV.h, absr_all)
                        g.wait_ge(sSP.h, attn_sp)
                    sWO.inc(g.dma_start(
                        out=WOS[:, :],
                        in_=wo_t[sl].transpose([1, 0, 2])), 16)
                wo_ld = sWO.n

                for c in range(NT):
                    @blk.sync
                    def _(sy, c=c, sl=sl):
                        if c >= 2 or sl > 0:
                            sy.wait_ge(sT.h, sT.n)
                        else:
                            sy.wait_ge(sV.h, absr_all)
                            sy.wait_ge(sSP.h, attn_sp)
                        sCX[c % 2].inc(sy.dma_start(
                            out=CTXC[c % 2][:, :],
                            in_=ctx_h[c].transpose([1, 0, 2])), 16)
                    ctx_ld = sCX[c % 2].n

                    @blk.tensor
                    def _(t, sl=sl, c=c, ctx_ld=ctx_ld, wo_ld=wo_ld, x_pre=x_pre,
                          bo_cast=bo_cast):
                        t.wait_ge(sCX[c % 2].h, ctx_ld)
                        t.wait_ge(sWO.h, wo_ld)
                        if c >= 2 or sl > 0:
                            t.wait_ge(sV.h, sV.n)
                        po = POW[c % 2]
                        for h in range(H):
                            ins = t.matmul(po[:, :], CTXC[c % 2][:, h*128:(h+1)*128],
                                           WOS[:, h*512:(h+1)*512],
                                           start=(h == 0), stop=(h == H - 1))
                        sT.inc(ins)
                    mmd = sT.n

                    @blk.vector
                    def _(v, sl=sl, c=c, mmd=mmd, bo_cast=bo_cast):
                        v.wait_ge(sT.h, mmd)
                        if sl == 0 and c == 0:
                            v.wait_ge(sV.h, bo_cast)
                        xs = XRES[:, c*2048 + sl*512: c*2048 + (sl+1)*512]
                        sV.inc(v.tensor_tensor(xs, xs, POW[c % 2][:, :], ALU.add))
        x1_v = sV.n


        ffn_id = [0]

        def emit_ffn(x_ready_v, XB, tile_hook=None):
            fid = ffn_id[0]
            ffn_id[0] += 1
            NG = 16
            NPT = NT // NG
            with ExitStack() as stF:
                EF = stF.enter_context
                XQT = [[EF(nc.sbuf_tensor(f"XQT{fid}_{b}_{c}", [128, 128], BF16))
                        for c in range(NCH)] for b in range(2)]
                FW = [EF(nc.sbuf_tensor(f"FW{fid}_{i}", [128, NCH * 512], BF16))
                      for i in range(2)]
                Q8A = EF(nc.sbuf_tensor(f"Q8A{fid}", [128, 2048], FP8))
                HIB = EF(nc.sbuf_tensor(f"HIB{fid}", [128, 2048], BF16))
                SCR8 = EF(nc.sbuf_tensor(f"SCR8{fid}", [128, 2048], U8))
                XQB = EF(nc.sbuf_tensor(f"XQB{fid}", [128, 2048], BF16))
                PTF = [EF(nc.psum_tensor(f"PTF{fid}_{i}", [128, 128], BF16))
                       for i in range(4)]
                PFW = [EF(nc.psum_tensor(f"PFW{fid}_{i}", [128, 512], F32))
                       for i in range(2)]
                Z2 = SCRAP
                sFW = [Sem(st, nc, f"sFW{fid}_{i}") for i in range(2)]
                sFWs2 = [Sem(st, nc, f"sFWsy{fid}_{i}") for i in range(2)]
                fwmm = {}
                ev_d = {}
                HFW = NCH // 2 * 512

                st_done = [0] * NT
                for i in range(NT):
                    @blk.vector
                    def _(v, i=i, x_ready_v=x_ready_v):
                        if i == 0:
                            v.wait_ge(sV.h, x_ready_v)
                        ln_stats_tile(v, XB[:, i*2048:(i+1)*2048], i)
                        sV.inc(v.engine_nop())
                    st_done[i] = sV.n

                @blk.vector
                def _(v):
                    ln_finalize(v, 16, rms2=False, fold_recip_s=True)
                    sV.inc(v.engine_nop())
                fin = sV.n

                def emit_qdq_tile(grp, ii):
                    if True:
                        i = grp * NPT + ii

                        @blk.vector
                        def _(v, i=i, ii=ii, grp=grp):
                            v.wait_ge(sV.h, max(fin, sV.n))
                            v.wait_ge(sT.h, sT.n)
                            v = AutoV(v)
                            v.tensor_scalar(Z2[:, :], XB[:, i*2048:(i+1)*2048],
                                            NSC[:, i:i+1], NBI[:, i:i+1],
                                            ALU.mult, ALU.add)
                            v.tensor_scalar(Z2[:, :], Z2[:, :], 448.0, -448.0,
                                            ALU.min, ALU.max)
                            v.tensor_copy(Q8A[:, :], Z2[:, :])
                            v.tensor_scalar(SCR8[:, :].bitcast(FP8), Z2[:, :], 0.5,
                                            None, ALU.mult)
                            v.tensor_scalar(HIB[:, :], SCR8[:, :].bitcast(FP8), 2.0,
                                            None, ALU.mult)
                            v.tensor_copy(XQB[:, :], Q8A[:, :])
                            v.tensor_scalar(SCR8[:, :], Z2[:, :], 240.0, None,
                                            ALU.is_gt)
                            v.copy_predicated(XQB[:, :], SCR8[:, :], HIB[:, :])
                            v.tensor_scalar(SCR8[:, :], Z2[:, :], -240.0, None,
                                            ALU.is_lt)
                            v.copy_predicated(XQB[:, :], SCR8[:, :], HIB[:, :])
                        xq_done = sV.n
                        base_v = sV.n

                        @blk.tensor
                        def _(t, ii=ii, xq_done=xq_done, base_v=base_v):
                            t.wait_ge(sV.h, xq_done)
                            for c in range(NCH):
                                if c >= 4:
                                    t.wait_ge(sV.h, base_v + c - 3)
                                sT.inc(t.transpose(PTF[c % 4][:, :],
                                                   XQB[:, c*128:(c+1)*128],
                                                   IDENT[:, :]))
                        tr = sT.n

                        @blk.vector
                        def _(v, ii=ii, grp=grp, tr=tr):
                            for c in range(NCH):
                                v.wait_ge(sT.h, tr - 15 + c)
                                sV.inc(v.tensor_copy(
                                    XQT[grp % 2][c][:, ii*128:(ii+1)*128],
                                    PTF[c % 4][:, :]))
                    return sV.n

                def emit_mm_group(grp, xq_grp):
                    for sl in range(4):
                        s = grp * 4 + sl
                        if grp + 1 < NG and sl < NPT:
                            emit_qdq_tile(grp + 1, sl)

                        @blk.gpsimd
                        def _(g, s=s, sl=sl, xq_grp=xq_grp):
                            if s <= 1:
                                if s == 0 and fid == 0:
                                    g.wait_ge(sFC.h, fc_all)
                                g.wait_ge(sV.h, xq_grp)
                                g.wait_ge(sT.h, sT.n)
                                g.wait_ge(sS.h, sS.n)
                            else:
                                g.wait_ge(sT.h, fwmm[s - 2])
                            sFW[s % 2].inc(g.dma_start(
                                out=FW[s % 2][:, 0:HFW],
                                in_=fw_bf[sl][:, 0:HFW]), 16)
                        f8_ld = sFW[s % 2].n

                        @blk.sync
                        def _(sy, s=s, sl=sl, xq_grp=xq_grp):
                            if s <= 1:
                                if s == 0 and fid == 0:
                                    sy.wait_ge(sFC.h, fc_all)
                                sy.wait_ge(sV.h, xq_grp)
                                sy.wait_ge(sT.h, sT.n)
                                sy.wait_ge(sS.h, sS.n)
                            else:
                                sy.wait_ge(sT.h, fwmm[s - 2])
                            sFWs2[s % 2].inc(sy.dma_start(
                                out=FW[s % 2][:, HFW:2 * HFW],
                                in_=fw_bf[sl][:, HFW:2 * HFW]), 16)
                        f8_lds = sFWs2[s % 2].n

                        for ii in range(NPT):
                            i = grp * NPT + ii

                            @blk.tensor
                            def _(t, ii=ii, s=s, sl=sl, grp=grp, f8_ld=f8_ld,
                                  f8_lds=f8_lds, xq_grp=xq_grp):
                                t.wait_ge(sFW[s % 2].h, f8_ld)
                                t.wait_ge(sFWs2[s % 2].h, f8_lds)
                                t.wait_ge(sV.h, xq_grp)
                                if s >= 2:
                                    t.wait_ge(sV.h, ev_d[s - 2])
                                for c in range(NCH):
                                    ins = t.matmul(PFW[s % 2][:, :],
                                                   XQT[grp % 2][c][:, ii*128:(ii+1)*128],
                                                   FW[s % 2][:, c*512:(c+1)*512],
                                                   start=(c == 0),
                                                   stop=(c == NCH - 1))
                                sT.inc(ins)
                            mmd = sT.n
                            fwmm[s] = sT.n

                            @blk.vector
                            def _(v, i=i, s=s, sl=sl, mmd=mmd):
                                v.wait_ge(sT.h, mmd)
                                xs = XB[:, i*2048 + sl*512: i*2048 + (sl+1)*512]
                                sV.inc(v.scalar_tensor_tensor(
                                    xs, PFW[s % 2][:, :], SCL[:, 1:2], xs,
                                    ALU.mult, ALU.add))
                            ev_d[s] = sV.n
                            if sl == 3 and tile_hook is not None:
                                tile_hook(i)

                for ii in range(NPT):
                    emit_qdq_tile(0, ii)
                xq = sV.n
                for grp in range(NG):
                    emit_mm_group(grp, xq)
                    xq = sV.n
            return sV.n

        if stage == 1:
            @blk.sync
            def _(sy):
                sy.wait_ge(sV.h, x1_v)
                for i in range(NT):
                    sOUT.inc(sy.dma_start(out=out_t[i],
                                          in_=XRES[:, i*2048:(i+1)*2048]), 16)
                sy.wait_ge(sOUT.h, sOUT.n)
            stX.close()
            return nc

        c1_done = [0]

        def _c1_hook(i):
            @blk.vector
            def _(v, i=i):
                v.wait_ge(sV.h, sV.n)
                ln_stats_tile(v, XRES[:, i*2048:(i+1)*2048], i)
                sV.inc(v.engine_nop())
            c1_done[0] = sV.n

        x2_v = emit_ffn(x1_v, XRES, tile_hook=_c1_hook)

        if stage == 2:
            @blk.sync
            def _(sy):
                sy.wait_ge(sV.h, x2_v)
                for i in range(NT):
                    sOUT.inc(sy.dma_start(out=out_t[i],
                                          in_=XRES[:, i*2048:(i+1)*2048]), 16)
                sy.wait_ge(sOUT.h, sOUT.n)
            stX.close()
            return nc

        sXS = Sem(st, nc, "sXS")

        @blk.sync
        def _(sy):
            sy.wait_ge(sV.h, x2_v)
            for i in range(NT):
                sXS.inc(sy.dma_start(out=x2_h[i], in_=XRES[:, i*2048:(i+1)*2048]), 16)
        x2sp = sXS.n
        stX.close()

        x3_dst = out_t if stage == 3 else x3_h

        stCC = ExitStack()
        SHALF = stCC.enter_context(nc.sbuf_tensor("SHALF", [128, H * 128], F32))
        SINIT = stCC.enter_context(nc.sbuf_tensor("SINIT", [128, H * 128], F32))
        RS = stCC.enter_context(nc.sbuf_tensor("RS", [128, NT], F32))
        SCLQ = stCC.enter_context(nc.sbuf_tensor("SCLQ", [128, 1], F32))
        MQH = stCC.enter_context(nc.sbuf_tensor("MQH", [128, 1], F32))
        SCLP = stCC.enter_context(nc.sbuf_tensor("SCLP", [128, 1], F32))
        MPH = stCC.enter_context(nc.sbuf_tensor("MPH", [128, 1], F32))
        NEGH = stCC.enter_context(nc.sbuf_tensor("NEGH", [128, 1], F32))
        POSH = stCC.enter_context(nc.sbuf_tensor("POSH", [128, 1], F32))
        ONESF = stCC.enter_context(nc.sbuf_tensor("ONESF", [128, 1], F32))
        sQW = [Sem(st, nc, f"sQW{i}") for i in range(2)]
        sQS = Sem(st, nc, "sQS")
        sQWs = [Sem(st, nc, f"sQWs{i}") for i in range(2)]
        sQT = Sem(st, nc, "sQT")
        sQL = [Sem(st, nc, f"sQL{i}") for i in range(4)]
        sX2 = [Sem(st, nc, f"sX2{i}") for i in range(2)]
        sPW = Sem(st, nc, "sPW")
        sX3 = Sem(st, nc, "sX3")

        @blk.gpsimd
        def _(g):
            g.wait_ge(sXS.h, x2sp)
            g.memset(NEGH[:, :], -0.5)
            g.memset(POSH[:, :], 0.5)
            sG.inc(g.memset(ONESF[:, :], 1.0))
        consts_g = sG.n

        with ExitStack() as stC:
            EC = stC.enter_context
            QWF = [EC(nc.sbuf_tensor("QWF0", [128, NCH * 384], F32))] * 2

            stB = ExitStack()
            PBC = stB.enter_context(nc.psum_tensor("PBC2", [128, 512], F32))
            @blk.vector
            def _(v):
                v.wait_ge(sV.h, max(absr_all, sV.n))
                v.wait_ge(sG.h, consts_g)
                v = AutoV(v)
                v.tensor_reduce(TMPC[:, 0:1], ABSD[:, 0:16], AX.X, ALU.add)
                v.tensor_reduce(TMPC[:, 1:2], ABSE[:, 0:8], AX.X, ALU.add)
            abs_v = sV.n

            @blk.tensor
            def _(t):
                t.wait_ge(sV.h, abs_v)
                sT.inc(t.matmul(PBC[0:1, 0:2], ONESF[:, :], TMPC[:, 0:2],
                                start=True, stop=True))
            abs_t = sT.n

            @blk.vector
            def _(v):
                v.wait_ge(sT.h, abs_t)
                v = AutoV(v)
                v.tensor_scalar(TMPC[0:1, 2:3], PBC[0:1, 0:1],
                                1.0 / (6144.0 * 2048.0), 1e-5, ALU.mult, ALU.max)
                v.tensor_scalar(TMPC[0:1, 3:4], PBC[0:1, 1:2],
                                1.0 / (2048.0 * 2048.0), 1e-5, ALU.mult, ALU.max)
                v.reciprocal(TMPC[0:1, 4:5], TMPC[0:1, 2:3])
                v.reciprocal(TMPC[0:1, 5:6], TMPC[0:1, 3:4])
            mq_v = sV.n

            @blk.tensor
            def _(t):
                t.wait_ge(sV.h, mq_v)
                sT.inc(t.matmul(PBC[:, 0:4], ONES1F[:, :], TMPC[0:1, 2:6],
                                start=True, stop=True))
            mq_t = sT.n

            @blk.vector
            def _(v):
                v.wait_ge(sT.h, mq_t)
                v = AutoV(v)
                v.tensor_scalar(MQH[:, :], PBC[:, 0:1], 0.5, None, ALU.mult)
                v.tensor_scalar(MPH[:, :], PBC[:, 1:2], 0.5, None, ALU.mult)
                v.tensor_copy(SCLQ[:, :], PBC[:, 2:3])
                v.tensor_copy(SCLP[:, :], PBC[:, 3:4])
            mrep_v = sV.n

            stB.close()

            with ExitStack() as stG:
                EG = stG.enter_context
                GT = [EG(nc.sbuf_tensor(f"GT{c}", [128, TC], BF16)) for c in range(NCH)]
                PTC = [EG(nc.psum_tensor(f"PTC{i}", [128, 128], BF16)) for i in range(2)]

                with ExitStack() as stT2:
                    XT2 = [stT2.enter_context(nc.sbuf_tensor(f"XT2{i}", [128, D], F32))
                           for i in range(2)]
                    @blk.vector
                    def _(v):
                        v.wait_ge(sV.h, max(c1_done[0], mrep_v))
                        ln_finalize(v, 16, rms2=True)
                        sV.inc(v.engine_nop())
                    fin = sV.n

                    for i in range(NT):
                        @blk.sync
                        def _(sy, i=i, fin=fin):
                            if i == 0:
                                sy.wait_ge(sXS.h, x2sp)
                            sy.wait_ge(sV.h, max(fin, sV.n if i >= 1 else 0))
                            sX2[i % 2].inc(sy.dma_start(out=XT2[i % 2][:, :],
                                                        in_=x2_h[i]), 16)
                        ld = sX2[i % 2].n

                        @blk.vector
                        def _(v, i=i, ld=ld, fin=fin):
                            v.wait_ge(sX2[i % 2].h, ld)
                            if i >= 1:
                                v.wait_ge(sT.h, sT.n)
                            sV.inc(v.tensor_scalar(HBF[:, :], XT2[i % 2][:, :],
                                                   NSC[:, i:i+1], NBI[:, i:i+1],
                                                   ALU.mult, ALU.add))
                        nrm = sV.n
                        base_v = sV.n

                        @blk.tensor
                        def _(t, i=i, nrm=nrm, base_v=base_v):
                            t.wait_ge(sV.h, nrm)
                            for c in range(NCH):
                                if c >= 2:
                                    t.wait_ge(sV.h, base_v + c - 1)
                                sT.inc(t.transpose(PTC[c % 2][:, :],
                                                   HBF[:, c*128:(c+1)*128], IDENT[:, :]))
                        tr = sT.n

                        @blk.vector
                        def _(v, i=i, tr=tr):
                            for c in range(NCH):
                                v.wait_ge(sT.h, tr - 15 + c)
                                sV.inc(v.tensor_single_scalar(
                                    GT[c][:, i*128:(i+1)*128], PTC[c % 2][:, :],
                                    QNW[:, c:c+1], ALU.mult))
                gt_v = sV.n

                stC3 = ExitStack()
                E3 = stC3.enter_context
                QS1 = E3(nc.sbuf_tensor("QS1", [128, NCH * 384], BF16))
                QTRN2 = [E3(nc.sbuf_tensor(f"QTRN{i}", [128, NCH * 384], BF16))
                         for i in range(2)]
                QTH = E3(nc.sbuf_tensor("QTH", [128, TC], BF16))
                KTH = E3(nc.sbuf_tensor("KTH", [128, TC], BF16))
                KTOK = E3(nc.sbuf_tensor("KTOK", [128, TC], BF16))
                VTH = E3(nc.sbuf_tensor("VTH", [128, TC], BF16))
                PRJ3 = [E3(nc.psum_tensor(f"PRJ3{i}", [128, 512], F32)) for i in range(2)]
                PV3 = [E3(nc.psum_tensor(f"PV3{i}", [128, 128], F32)) for i in range(2)]
                PK3 = E3(nc.psum_tensor("PK3", [128, 128], F32))
                KLAMF3 = E3(nc.sbuf_tensor("KLAMF3", [128, TC], BF16))
                prev_qt = [0]
                prev_vproj = {0: 0, 1: 0}
                kv_t = {}
                shcp = {}
                klam_vd = {}

                def emit_kv(h):
                    @blk.tensor
                    def _(t, h=h):
                        t.wait_ge(sV.h, klam_vd[h])
                        if h >= 1:
                            t.wait_ge(sV.h, shcp[h - 1])
                        for j in range(NJ):
                            sT.inc(t.matmul(PK3[:, :],
                                            KLAMF3[:, j*128:(j+1)*128],
                                            VTH[:, j*128:(j+1)*128],
                                            start=(j == 0), stop=(j == NJ - 1)))
                    kv_t[h] = sT.n

                    @blk.vector
                    def _(v, h=h):
                        v.wait_ge(sT.h, kv_t[h])
                        sV.inc(v.tensor_copy(SHALF[:, h*128:(h+1)*128], PK3[:, :]))
                    shcp[h] = sV.n
                qk_sp = {}
                vk_sp = {}
                HHALF = NCH // 2 * 384
                for h in range(H):
                    par = h % 2

                    @blk.gpsimd
                    def _(g, h=h, gt_v=gt_v):
                        if h == 0:
                            g.wait_ge(sV.h, gt_v)
                            g.wait_ge(sXS.h, x2sp)
                        if h >= 1:
                            g.wait_ge(sS.h, sS.n)
                        sQW[h % 2].inc(g.dma_start(
                            out=QWF[h % 2][:, :],
                            in_=qkv_t[h].transpose([1, 0, 2])), 16)
                    qw_ld = sQW[h % 2].n

                    @blk.scalar
                    def _(s, h=h, par=par, qw_ld=qw_ld):
                        s.wait_ge(sQW[h % 2].h, qw_ld)
                        if h >= 2:
                            s.wait_ge(sT.h, prev_vproj[par])
                        else:
                            s.wait_ge(sT.h, sT.n)
                        if h >= 1:
                            s.wait_ge(sV.h, prev_qt[0])
                        else:
                            s.wait_ge(sV.h, sV.n)
                        s.activation(QTRN2[par][:, :], QWF[h % 2][:, :], AF.Sign,
                                     bias=NEGH[:, 0:1], scale=SCLQ[:, 0:1])
                        sS.inc(s.activation(QS1[:, :], QWF[h % 2][:, :], AF.Sign,
                                            bias=POSH[:, 0:1], scale=SCLQ[:, 0:1]))
                    sgn = sS.n

                    @blk.vector
                    def _(v, par=par, sgn=sgn):
                        v.wait_ge(sS.h, sgn)
                        sV.inc(v.tensor_tensor(QTRN2[par][:, :], QTRN2[par][:, :],
                                               QS1[:, :], ALU.add))
                    qt_v = sV.n
                    prev_qt[0] = qt_v

                    if h >= 1:
                        emit_kv(h - 1)

                    for oi, DST in ((0, QTH), (1, KTH)):
                        base_ev = sV.n
                        for qs in range(4):
                            @blk.tensor
                            def _(t, par=par, oi=oi, qs=qs, qt_v=qt_v,
                                  base_ev=base_ev):
                                t.wait_ge(sV.h, max(qt_v, base_ev if qs < 2 else sV.n))
                                for c in range(NCH):
                                    ins = t.matmul(
                                        PRJ3[qs % 2][:, :],
                                        QTRN2[par][:, c*384+oi*128:c*384+(oi+1)*128],
                                        GT[c][:, qs*512:(qs+1)*512],
                                        start=(c == 0), stop=(c == NCH - 1))
                                sT.inc(ins)
                            mmd = sT.n

                            @blk.vector
                            def _(v, h=h, oi=oi, qs=qs, DST=DST, mmd=mmd):
                                v.wait_ge(sT.h, mmd)
                                if h >= 1 and oi == 0 and qs == 0:
                                    v.wait_ge(sQT.h, vk_sp[h - 1])
                                sV.inc(v.tensor_single_scalar(
                                    DST[:, qs*512:(qs+1)*512], PRJ3[qs % 2][:, :],
                                    MQH[:, 0:1], ALU.mult))
                        ev_d = sV.n

                        @blk.sync
                        def _(sy, h=h, oi=oi, DST=DST, ev_d=ev_d):
                            sy.wait_ge(sV.h, ev_d)
                            sQT.inc(sy.dma_start(out=qkv_h[h, oi], in_=DST[:, :]),
                                    16)
                        if oi == 1:
                            qk_sp[h] = sQT.n

                    vbase = sV.n
                    for i in range(NT):
                        @blk.tensor
                        def _(t, par=par, i=i, qt_v=qt_v, vbase=vbase):
                            t.wait_ge(sV.h, max(qt_v, vbase if i < 2 else sV.n))
                            for c in range(NCH):
                                ins = t.matmul(PV3[i % 2][:, :],
                                               GT[c][:, i*128:(i+1)*128],
                                               QTRN2[par][:, c*384+256:c*384+384],
                                               start=(c == 0), stop=(c == NCH - 1))
                            sT.inc(ins)
                        mmd = sT.n

                        @blk.vector
                        def _(v, i=i, mmd=mmd):
                            v.wait_ge(sT.h, mmd)
                            sV.inc(v.tensor_single_scalar(
                                VTH[:, i*128:(i+1)*128], PV3[i % 2][:, :],
                                MQH[:, 0:1], ALU.mult))
                    vth_v = sV.n
                    prev_vproj[par] = sT.n

                    @blk.sync
                    def _(sy, h=h, vth_v=vth_v):
                        sy.wait_ge(sV.h, vth_v)
                        sQT.inc(sy.dma_start(out=qkv_h[h, 3], in_=VTH[:, :]), 16)

                    tbase = sV.n
                    for c in range(NT):
                        @blk.tensor
                        def _(t, c=c, vth_v=vth_v, tbase=tbase):
                            t.wait_ge(sV.h, max(vth_v, tbase if c < 2 else sV.n))
                            sT.inc(t.transpose(PTC[c % 2][:, :],
                                               KTH[:, c*128:(c+1)*128], IDENT[:, :]))
                        trd = sT.n

                        @blk.vector
                        def _(v, c=c, trd=trd):
                            v.wait_ge(sT.h, trd)
                            sV.inc(v.tensor_copy(KTOK[:, c*128:(c+1)*128],
                                                 PTC[c % 2][:, :]))
                    ktok_v = sV.n

                    @blk.sync
                    def _(sy, h=h, ktok_v=ktok_v):
                        sy.wait_ge(sV.h, ktok_v)
                        sQT.inc(sy.dma_start(out=qkv_h[h, 2], in_=KTOK[:, :]), 16)
                    vk_sp[h] = sQT.n

                    @blk.vector
                    def _(v, h=h, ktok_v=ktok_v):
                        v.wait_ge(sV.h, max(ktok_v, sV.n))
                        if h >= 1:
                            v.wait_ge(sT.h, kv_t[h - 1])
                        kl3 = KLAMF3[:, :].rearrange("p (j d) -> p j d", d=128)
                        kt3 = KTOK[:, :].rearrange("p (j d) -> p j d", d=128)
                        lj_b = LAMJ[:, h*NJ:(h+1)*NJ].unsqueeze(2) \
                            .broadcast_to([128, NJ, 128])
                        sV.inc(v.scalar_tensor_tensor(
                            kl3, kt3, LAMRV[:, h:h+1], lj_b,
                            ALU.mult, ALU.mult))
                    klam_vd[h] = sV.n
                emit_kv(H - 1)
                c3_sp = sQT.n
                c3_v = sV.n
                stC3.close()

            @blk.gpsimd
            def _(g, c3_sp=c3_sp, c3_v=c3_v):
                g.wait_ge(sQT.h, c3_sp)
                g.wait_ge(sV.h, c3_v)
                sQS.inc(g.dma_start(out=cc_in.ap(), in_=SHALF[:, :]), 16)
                g.wait_ge(sQS.h, sQS.n)
                g.collective_compute(
                    "AllGather", ALU.bypass,
                    replica_groups=[[0, 1], [2, 3], [4, 5], [6, 7]],
                    ins=[cc_in.ap().opt()],
                    outs=[cc_out.ap().opt()],
                ).then_inc(sCC.h, 1)
                sCC.n += 1
                g.wait_ge(sCC.h, sCC.n)
                sQS.inc(g.dma_start(out=SINIT[:, :], in_=cc_out[0]), 16)
            cc_g = sQS.n

            @blk.vector
            def _(v, cc_g=cc_g):
                v.wait_ge(sQS.h, cc_g)
                sV.inc(v.tensor_single_scalar(SINIT[:, :], SINIT[:, :], SMK[:, 0:1],
                                              ALU.mult))
            s0_v = sV.n

        STR = stCC.enter_context(nc.sbuf_tensor("STR", [2, 2048], F32))
        with ExitStack() as stS:
            ES = stS.enter_context
            OALL = [ES(nc.sbuf_tensor(f"OALL{h}", [128, TC], BF16)) for h in range(H)]
            stS2 = ExitStack()
            ES2 = stS2.enter_context
            NB = 4
            QKVL = [ES2(nc.sbuf_tensor(f"QKVL{i}", [128, 4 * TC], BF16))
                    for i in range(NB)]
            SB16 = [ES2(nc.sbuf_tensor(f"SB16_{i}", [128, 128], BF16))
                    for i in range(2)]
            E2 = [ES2(nc.sbuf_tensor(f"E2_{i}", [128, 128], BF16)) for i in range(2)]
            QLAMF = [ES2(nc.sbuf_tensor(f"QLAMF{i}", [128, TC], BF16))
                     for i in range(2)]
            KLAMF = [ES2(nc.sbuf_tensor(f"KLAMF{i}", [128, TC], BF16))
                     for i in range(2)]
            OSQ = ES2(nc.sbuf_tensor("OSQ", [128, TC], BF16))
            SACC = ES2(nc.sbuf_tensor("SACC", [2, 2048], F32))
            with ExitStack() as stP:
                PSC = [stP.enter_context(nc.psum_tensor(f"PSC{i}", [128, 128], F32))
                       for i in range(2)]
                PO5 = [stP.enter_context(nc.psum_tensor(f"PO5{i}", [128, 128], F32))
                       for i in range(2)]
                PK5 = [stP.enter_context(nc.psum_tensor(f"PK5{i}", [128, 128], F32))
                       for i in range(2)]
                PSTQ = stP.enter_context(nc.psum_tensor("PSTQ", [2, 1024], F32))
                c5_t0 = sT.n
                c5_s0 = sS.n

                @blk.vector
                def _(v):
                    v.wait_ge(sT.h, sT.n)
                    v.wait_ge(sS.h, sS.n)
                    sV.inc(v.memset(SACC[:, :], 0.0))
                sacc0_v = sV.n

                qld = {}
                init_v = {}
                psct = {}
                pkt = {}
                pot = {}
                e2v = {}
                sttv = {}
                oevs = {}
                osq_v = {}
                stats_t = {}
                pstq_free_v = 0

                def emit_load(h):
                    @blk.sync
                    def _(sy, h=h):
                        if h == 0:
                            sy.wait_ge(sV.h, s0_v)
                            sy.wait_ge(sT.h, c5_t0)
                            sy.wait_ge(sS.h, c5_s0)
                        elif h >= NB:
                            sy.wait_ge(sT.h, pot[(h - NB, NJ - 1)])
                        sQL[h % NB].inc(sy.dma_start(
                            out=QKVL[h % NB][:, :],
                            in_=qkv_h[h].transpose([1, 0, 2])), 16)
                    qld[h] = sQL[h % NB].n

                def emit_init(h):
                    par = h % 2
                    qb = QKVL[h % NB]
                    Q5 = qb[:, 0:2048]
                    KT5 = qb[:, 4096:6144]

                    @blk.vector
                    def _(v, h=h, par=par, Q5=Q5, KT5=KT5):
                        v.wait_ge(sQL[h % NB].h, qld[h])
                        if h >= 2:
                            v.wait_ge(sT.h, pot[(h - 2, NJ - 1)])
                            v.wait_ge(sV.h, sttv[(h - 2, NJ - 1)])
                        else:
                            v.wait_ge(sT.h, c5_t0)
                            v.wait_ge(sS.h, c5_s0)
                            v.wait_ge(sV.h, s0_v)
                        sV.inc(v.tensor_copy(SB16[par][:, :],
                                             SINIT[:, h*128:(h+1)*128]))
                        lam_b = LAMQ[:, h*128:(h+1)*128].unsqueeze(1) \
                            .broadcast_to([128, NJ, 128])
                        q3 = Q5.rearrange("p (j t) -> p j t", t=128)
                        qo = QLAMF[par][:, :].rearrange("p (j t) -> p j t", t=128)
                        sV.inc(v.tensor_tensor(qo, q3, lam_b, ALU.mult))
                        sV.inc(v.tensor_single_scalar(KLAMF[par][:, :], KT5,
                                                      LAMRV[:, h:h+1], ALU.mult))
                    init_v[h] = sV.n

                mm_t = {}
                adds_v = {}

                def emit_osq(h):
                    @blk.vector
                    def _(v, h=h):
                        v.wait_ge(sS.h, oevs[(h, NJ - 1)])
                        if h >= 1:
                            v.wait_ge(sT.h, stats_t[h - 1])
                        else:
                            v.wait_ge(sT.h, c5_t0)
                        sV.inc(v.tensor_tensor(OSQ[:, :], OALL[h][:, :],
                                               OALL[h][:, :], ALU.mult))
                    osq_v[h] = sV.n

                def emit_stats_mm(h, r):
                    @blk.tensor
                    def _(t, h=h, r=r):
                        if r == 0:
                            t.wait_ge(sV.h, osq_v[h])
                            if h >= 1:
                                t.wait_ge(sV.h, adds_v[(h - 1, 1)])
                        else:
                            t.wait_ge(sV.h, adds_v[(h, 0)])
                        for half in range(2):
                            qs = 2 * r + half
                            sT.inc(t.matmul(
                                PSTQ[:, half*512:half*512 + 512],
                                W1SQ[:, 2*h:2*h+2],
                                OSQ[:, qs*512:(qs+1)*512],
                                start=True, stop=True))
                    mm_t[(h, r)] = sT.n
                    if r == 1:
                        stats_t[h] = sT.n

                def emit_adds(h, r):
                    nonlocal pstq_free_v

                    @blk.vector
                    def _(v, h=h, r=r):
                        v.wait_ge(sT.h, mm_t[(h, r)])
                        if h == 0 and r == 0:
                            v.wait_ge(sV.h, sacc0_v)
                        for half in range(2):
                            qs = 2 * r + half
                            sV.inc(v.tensor_tensor(
                                SACC[:, qs*512:(qs+1)*512],
                                SACC[:, qs*512:(qs+1)*512],
                                PSTQ[:, half*512:half*512 + 512],
                                ALU.add))
                    adds_v[(h, r)] = sV.n
                    pstq_free_v = sV.n

                def emit_scan_front(h, j):
                    par = h % 2
                    qb = QKVL[h % NB]
                    Q5 = qb[:, 0:2048]
                    K5 = qb[:, 2048:4096]
                    V5 = qb[:, 6144:8192]

                    @blk.tensor
                    def _(t, h=h, par=par, j=j, Q5=Q5, K5=K5):
                        if j == 0:
                            t.wait_ge(sV.h, init_v[h])
                        else:
                            t.wait_ge(sV.h, e2v[(h, j-1)])
                        sT.inc(t.matmul(PSC[par][:, :],
                                        K5[:, j*128:(j+1)*128],
                                        Q5[:, j*128:(j+1)*128],
                                        start=True, stop=True))
                    psct[(h, j)] = sT.n

                    @blk.tensor
                    def _(t, h=h, par=par, j=j, V5=V5):
                        if j >= 1:
                            t.wait_ge(sV.h, sttv[(h, j-1)])
                        sT.inc(t.matmul(PK5[par][:, :],
                                        KLAMF[par][:, j*128:(j+1)*128],
                                        V5[:, j*128:(j+1)*128],
                                        start=True, stop=True))
                    pkt[(h, j)] = sT.n

                    @blk.vector
                    def _(v, h=h, par=par, j=j):
                        v.wait_ge(sT.h, psct[(h, j)])
                        sV.inc(v.tensor_tensor(E2[par][:, :], PSC[par][:, :],
                                               MASKT[:, h*128:(h+1)*128],
                                               ALU.mult))
                    e2v[(h, j)] = sV.n

                def emit_scan_back(h, j):
                    par = h % 2
                    qb = QKVL[h % NB]
                    V5 = qb[:, 6144:8192]

                    @blk.tensor
                    def _(t, h=h, par=par, j=j, V5=V5):
                        t.wait_ge(sV.h, e2v[(h, j)])
                        if j >= 1:
                            t.wait_ge(sV.h, sttv[(h, j-1)])
                            t.wait_ge(sS.h, oevs[(h, j-1)])
                        elif h >= 2:
                            t.wait_ge(sS.h, oevs[(h - 2, NJ - 1)])
                        t.matmul(PO5[par][:, :], V5[:, j*128:(j+1)*128],
                                 E2[par][:, :], start=True, stop=False)
                        sT.inc(t.matmul(PO5[par][:, :], SB16[par][:, :],
                                        QLAMF[par][:, j*128:(j+1)*128],
                                        start=False, stop=True))
                    pot[(h, j)] = sT.n

                    @blk.vector
                    def _(v, h=h, par=par, j=j):
                        v.wait_ge(sT.h, pot[(h, j)])
                        v.wait_ge(sT.h, pkt[(h, j)])
                        sV.inc(v.scalar_tensor_tensor(
                            SB16[par][:, :], SB16[par][:, :],
                            LAMTT[:, h:h+1], PK5[par][:, :],
                            ALU.mult, ALU.add))
                    sttv[(h, j)] = sV.n

                    @blk.scalar
                    def _(s, h=h, par=par, j=j):
                        s.wait_ge(sT.h, pot[(h, j)])
                        sS.inc(s.activation(OALL[h][:, j*128:(j+1)*128],
                                            PO5[par][:, :], AF.Copy,
                                            scale=OSCALE))
                    oevs[(h, j)] = sS.n

                for h in range(NB):
                    emit_load(h)
                emit_init(0)
                emit_init(1)

                for hp in range(H // 2):
                    hA, hB = 2 * hp, 2 * hp + 1
                    pA, pB = hA - 2, hB - 2
                    for j in range(NJ):
                        if hp >= 1:
                            if j == 2:
                                emit_osq(pA)
                                emit_stats_mm(pA, 0)
                            elif j == 4:
                                emit_adds(pA, 0)
                                emit_stats_mm(pA, 1)
                            elif j == 6:
                                emit_adds(pA, 1)
                                emit_osq(pB)
                                emit_stats_mm(pB, 0)
                            elif j == 8:
                                emit_adds(pB, 0)
                                emit_stats_mm(pB, 1)
                            elif j == 10:
                                emit_adds(pB, 1)
                        emit_scan_front(hA, j)
                        emit_scan_front(hB, j)
                        emit_scan_back(hA, j)
                        emit_scan_back(hB, j)
                    for h in (hA + NB, hB + NB):
                        if h < H:
                            emit_load(h)
                    for h in (hA + 2, hB + 2):
                        if h < H:
                            emit_init(h)
                for h in (H - 2, H - 1):
                    emit_osq(h)
                    emit_stats_mm(h, 0)
                    emit_adds(h, 0)
                    emit_stats_mm(h, 1)
                    emit_adds(h, 1)

                @blk.vector
                def _(v):
                    v.wait_ge(sV.h, pstq_free_v)
                    sV.inc(v.tensor_copy(STR[:, :], SACC[:, :]))
                str_v = sV.n
            stS2.close()

            with ExitStack() as stP2:
                PTS = stP2.enter_context(nc.psum_tensor("PTS", [128, 2], F32))
                for c in range(NT):
                    @blk.tensor
                    def _(t, c=c, str_v=str_v):
                        t.wait_ge(sV.h, str_v if c == 0 else sV.n)
                        sT.inc(t.transpose(PTS[:, :], STR[0:2, c*128:(c+1)*128],
                                           IDENTF[0:2, 0:2]))
                    pt_t = sT.n

                    @blk.vector
                    def _(v, c=c, pt_t=pt_t):
                        v.wait_ge(sT.h, pt_t)
                        sV.inc(v.tensor_copy(TMPD[:, c:c+1], PTS[:, 0:1]))
                        v.wait_ge(sV.h, sV.n)
                        sV.inc(v.tensor_copy(TMPE[:, c:c+1], PTS[:, 1:2]))

                @blk.vector
                def _(v):
                    v.wait_ge(sV.h, sV.n)
                    v = AutoV(v)
                    v.tensor_scalar(TMPA[:, 0:16], TMPD[:, 0:16], 1.0 / D, None,
                                    ALU.mult)
                    v.tensor_single_scalar(TMPA[:, 0:16], TMPA[:, 0:16],
                                           CONSC[:, 4:5], ALU.add)
                    rsqrt(v, TMPB[:, 0:16], TMPA[:, 0:16], TMPC[:, 0:16],
                          STATA[:, 0:16])
                    v.tensor_tensor(TMPC[:, 0:16], TMPB[:, 0:16], TMPB[:, 0:16],
                                    ALU.mult)
                    v.tensor_scalar(TMPD[:, 0:16], TMPE[:, 0:16], 1.0 / D, None,
                                    ALU.mult)
                    v.tensor_tensor(TMPC[:, 0:16], TMPC[:, 0:16], TMPD[:, 0:16],
                                    ALU.mult)
                    v.tensor_single_scalar(TMPC[:, 0:16], TMPC[:, 0:16],
                                           CONSC[:, 3:4], ALU.add)
                    rsqrt(v, TMPD[:, 0:16], TMPC[:, 0:16], TMPA[:, 0:16],
                          STATA[:, 0:16])
                    v.tensor_tensor(RS[:, 0:16], TMPB[:, 0:16], TMPD[:, 0:16],
                                    ALU.mult)
                    v.tensor_single_scalar(RS[:, 0:16], RS[:, 0:16], MPH[:, 0:1],
                                           ALU.mult)
                    for h in range(H):
                        v.tensor_single_scalar(OALL[h][:, :], OALL[h][:, :],
                                               W12S[:, h:h+1], ALU.mult)
                rs_v = sV.n

            with ExitStack() as stJ:
                EJ = stJ.enter_context
                PJW = EJ(nc.sbuf_tensor("PJW", [128, NCH * 512], F32))
                PS1 = EJ(nc.sbuf_tensor("PS1", [128, NCH * 512], BF16))
                PJQ = EJ(nc.sbuf_tensor("PJQ", [128, NCH * 512], BF16))
                X2SL = [EJ(nc.sbuf_tensor(f"X2SL{i}", [128, 512], F32)) for i in range(2)]
                X3T = [EJ(nc.sbuf_tensor(f"X3T{i}", [128, 512], F32)) for i in range(2)]
                PF3 = [EJ(nc.psum_tensor(f"PF3{i}", [128, 512], F32)) for i in range(2)]

                for sl in range(4):
                    @blk.gpsimd
                    def _(g, sl=sl, rs_v=rs_v):
                        if sl == 0:
                            g.wait_ge(sV.h, rs_v)
                        else:
                            g.wait_ge(sS.h, sS.n)
                        sPW.inc(g.dma_start(out=PJW[:, :],
                                            in_=proj_t[sl].transpose([1, 0, 2])), 16)
                    pw_ld = sPW.n

                    @blk.scalar
                    def _(s, sl=sl, pw_ld=pw_ld):
                        s.wait_ge(sPW.h, pw_ld)
                        if sl >= 1:
                            s.wait_ge(sV.h, sV.n)
                        s.activation(PS1[:, :], PJW[:, :], AF.Sign,
                                     bias=NEGH[:, 0:1], scale=SCLP[:, 0:1])
                        sS.inc(s.activation(PJQ[:, :], PJW[:, :], AF.Sign,
                                            bias=POSH[:, 0:1], scale=SCLP[:, 0:1]))
                    psgn = sS.n

                    @blk.vector
                    def _(v, sl=sl, psgn=psgn):
                        v.wait_ge(sS.h, psgn)
                        if sl >= 1:
                            v.wait_ge(sT.h, sT.n)
                        sV.inc(v.tensor_tensor(PJQ[:, :], PJQ[:, :], PS1[:, :],
                                               ALU.add))
                    pjq_v = sV.n

                    for c in range(NT):
                        @blk.sync
                        def _(sy, sl=sl, c=c, rs_v=rs_v):
                            if c == 0 and sl == 0:
                                sy.wait_ge(sV.h, rs_v)
                            if c >= 2 or sl > 0:
                                sy.wait_ge(sV.h, sV.n)
                            sX2[c % 2].inc(sy.dma_start(
                                out=X2SL[c % 2][:, :],
                                in_=x2_h[c][:, sl*512:(sl+1)*512]), 16)
                        xld = sX2[c % 2].n

                        @blk.tensor
                        def _(t, sl=sl, c=c, pjq_v=pjq_v):
                            t.wait_ge(sV.h, max(pjq_v,
                                                sV.n if (c >= 2 or sl > 0) else 0))
                            for hh in range(H):
                                ins = t.matmul(PF3[c % 2][:, :],
                                               OALL[hh][:, c*128:(c+1)*128],
                                               PJQ[:, hh*512:(hh+1)*512],
                                               start=(hh == 0), stop=(hh == H - 1))
                            sT.inc(ins)
                        mmd = sT.n

                        @blk.vector
                        def _(v, sl=sl, c=c, mmd=mmd, xld=xld):
                            v.wait_ge(sT.h, mmd)
                            v.wait_ge(sX2[c % 2].h, xld)
                            if c >= 2 or sl > 0:
                                v.wait_ge(sX3.h, sX3.n)
                            v.wait_ge(sV.h, sV.n)
                            sV.inc(v.scalar_tensor_tensor(
                                X3T[c % 2][:, :], PF3[c % 2][:, :], RS[:, c:c+1],
                                X2SL[c % 2][:, :], ALU.mult, ALU.add))
                        ev3 = sV.n

                        @blk.sync
                        def _(sy, sl=sl, c=c, ev3=ev3):
                            sy.wait_ge(sV.h, ev3)
                            sX3.inc(sy.dma_start(out=x3_dst[c][:, sl*512:(sl+1)*512],
                                                 in_=X3T[c % 2][:, :]), 16)
        x3_stores = sX3.n
        stCC.close()

        if stage == 3:
            @blk.sync
            def _(sy):
                sy.wait_ge(sX3.h, x3_stores)
            return nc

        stX2 = ExitStack()
        XRES2 = stX2.enter_context(nc.sbuf_tensor("XRES2", [128, NT * 2048], F32))
        sXR = Sem(st, nc, "sXR")

        @blk.sync
        def _(sy):
            sy.wait_ge(sX3.h, x3_stores)
            for i in range(NT):
                sXR.inc(sy.dma_start(out=XRES2[:, i*2048:(i+1)*2048], in_=x3_h[i]), 16)
        x3l = sXR.n

        @blk.vector
        def _(v):
            v.wait_ge(sXR.h, x3l)
            sV.inc(v.engine_nop())
        x3_v = sV.n

        x4_v = emit_ffn(x3_v, XRES2)

        @blk.sync
        def _(sy):
            sy.wait_ge(sV.h, x4_v)
            for i in range(NT):
                sOUT.inc(sy.dma_start(out=out_t[i], in_=XRES2[:, i*2048:(i+1)*2048]),
                         16)
            sy.wait_ge(sOUT.h, sOUT.n)
        stX2.close()
        return nc

    return nc


def _host_layout(inputs, core):
    b, s = core // 2, core % 2
    x = np.asarray(inputs["L0_attn_a_x"])
    Wq = np.asarray(inputs["L0_attn_a_Wq"]); Wk = np.asarray(inputs["L0_attn_a_Wk"])
    Wv = np.asarray(inputs["L0_attn_a_Wv"]); Wo = np.asarray(inputs["L0_attn_a_Wo"])
    bo = np.asarray(inputs["L0_attn_a_bo"])
    qkv_w = np.asarray(inputs["L0_attn_b_qkv_w"])
    proj_w = np.asarray(inputs["L0_attn_b_proj_w"])
    f8 = np.asarray(inputs["L0_ffn_weight"])
    if f8.dtype != ml_dtypes.float8_e4m3fn:
        f8 = f8.view(ml_dtypes.float8_e4m3fn)

    d = {}
    d["xo"] = np.ascontiguousarray(x[b, s*2048:(s+1)*2048])
    d["xr"] = np.ascontiguousarray(x[b, (1-s)*2048:(2-s)*2048])
    wqT = Wq.T
    d["wq_t"] = np.ascontiguousarray(wqT.reshape(16, 128, 16, 128).transpose(2, 0, 1, 3))
    d["wkv_t"] = np.ascontiguousarray(
        np.concatenate([Wk.T, Wv.T], axis=1).reshape(16, 128, 256))
    d["wo_t"] = np.ascontiguousarray(Wo.T.reshape(16, 128, 4, 512).transpose(2, 0, 1, 3))
    d["bo"] = np.ascontiguousarray(np.broadcast_to(bo[None, :], (128, 2048)))
    qk = qkv_w.T.reshape(16, 128, 3, 16, 128)
    d["qkv_t"] = np.ascontiguousarray(
        qk.transpose(3, 0, 1, 2, 4).reshape(16, 16, 128, 384))
    d["proj_t"] = np.ascontiguousarray(
        proj_w.T.reshape(16, 128, 4, 512).transpose(2, 0, 1, 3))
    f8T = np.ascontiguousarray(f8.T)
    d["f8_t"] = np.ascontiguousarray(
        f8T.reshape(16, 128, 4, 512).transpose(2, 0, 1, 3)).view(ml_dtypes.float8_e4m3)
    d["wscale"] = np.ascontiguousarray(np.broadcast_to(
        np.asarray(inputs["L0_ffn_weight_scale"]).reshape(1, 1), (128, 1)))
    isc = np.asarray(inputs["L0_ffn_input_scale"])
    if isc.dtype != ml_dtypes.bfloat16:
        isc = isc.view(ml_dtypes.bfloat16)
    d["iscale"] = np.ascontiguousarray(np.broadcast_to(isc.reshape(1, 1), (128, 1)))
    d["s_cols"] = np.ascontiguousarray(
        np.broadcast_to(np.asarray(inputs["L0_attn_b_s"])[None, :], (128, 16)))
    d["qnw"] = np.ascontiguousarray(
        np.asarray(inputs["L0_attn_b_qkv_norm_w"]).reshape(16, 128).T)
    d["nw"] = np.ascontiguousarray(
        np.asarray(inputs["L0_attn_b_norm_w"]).reshape(16, 128).T)
    d["pnw"] = np.ascontiguousarray(
        np.asarray(inputs["L0_attn_b_proj_norm_w"]).reshape(16, 128).T)
    d["smask"] = np.full((128, 1), float(s), dtype=np.float32)
    return d


def kernel(**inputs):
    nc = build(STAGE)
    in_maps = [_host_layout(inputs, c) for c in range(8)]
    res = run_bass_kernel_spmd(nc, in_maps, core_ids=list(range(8)))
    out = np.empty((4, 4096, 2048), np.float32)
    for c in range(8):
        b, s = c // 2, c % 2
        out[b, s*2048:(s+1)*2048] = res.results[c]["out"]
    return out
